# version 11
# baseline (speedup 1.0000x reference)
import numpy as np
import ml_dtypes
from contextlib import ExitStack

import concourse.bass as bass
import concourse.mybir as mybir
from concourse.bass_utils import run_bass_kernel_spmd

F32 = mybir.dt.float32
BF16 = mybir.dt.bfloat16
ALU = mybir.AluOpType
ACT = mybir.ActivationFunctionType
AXX = mybir.AxisListType.X

B, S, D = 32, 2048, 1024
NCORES = 8
SEQ_PC = B // NCORES
TOK = SEQ_PC * S
NT = TOK // 128
TPS = S // 128
II = 2816
NIC = II // 128
NDC = D // 128
IHALF = II // 2
NH = IHALF // 128
EPS = 1e-5

_TH = (0.25, 0.75, 1.25, 1.75, 2.5, 3.5, 5.0)
TH2 = tuple(t * t for t in _TH)
WQ = (0.5, 0.5, 0.5, 0.5, 1.0, 1.0, 2.0)

ENGS = ("sync", "tensor", "vector", "scalar", "gpsimd")
import os as _os
DBG_SEQ = int(_os.environ.get("KDBG_SEQ", "0"))


class Sem:
    __slots__ = ("h", "v", "name")

    def __init__(self, h, name):
        self.h = h
        self.v = 0
        self.name = name


class Ph:

    def __init__(self):
        self.ops = {e: [] for e in ENGS}

    def on(self, eng, emit, waits=(), inc=None, inc_by=1):
        if inc is not None:
            inc.v += inc_by

        def f(e, emit=emit, waits=tuple(waits), inc=inc, inc_by=inc_by):
            for (s, v) in waits:
                if v > 0:
                    e.wait_ge(s.h, v)
            ins = emit(e)
            if inc is not None:
                ins.then_inc(inc.h, inc_by)
        self.ops[eng].append(f)

    def wait(self, eng, waits):
        def f(e, waits=tuple(waits)):
            for (s, v) in waits:
                if v > 0:
                    e.wait_ge(s.h, v)
        self.ops[eng].append(f)

    def run(self, nc):
        with nc.Block() as block:
            @block.sync
            def _(e):
                for f in self.ops["sync"]:
                    f(e)

            @block.tensor
            def _(e):
                for f in self.ops["tensor"]:
                    f(e)

            @block.vector
            def _(e):
                for f in self.ops["vector"]:
                    f(e)

            @block.scalar
            def _(e):
                for f in self.ops["scalar"]:
                    f(e)

            @block.gpsimd
            def _(e):
                for f in self.ops["gpsimd"]:
                    f(e)


def _b16(t, nb):
    return t[:, 0:nb].unsqueeze(2).broadcast_to([128, nb, 16])


def mk_qdq_bufs(nc, es, prefix, width):
    qb = {}
    for n in ("fv", "fv2"):
        qb[n] = es.enter_context(nc.sbuf_tensor(prefix + n, [128, width], F32))
    for n in ("cA", "cB", "t1", "t2", "t3", "sgn"):
        qb[n] = es.enter_context(nc.sbuf_tensor(prefix + n, [128, width], BF16))
    qb["thb"] = es.enter_context(
        nc.sbuf_tensor(prefix + "thb", [128, 2], F32))
    for n in ("bm", "sc", "rec", "rc2"):
        qb[n] = es.enter_context(nc.sbuf_tensor(prefix + n, [128, width // 16], F32))
    return qb


def emit_qdq_init(ph, qb):
    ph.on("vector", lambda e: e.memset(qb["thb"][:, 0:1], -TH2[4]))
    ph.on("vector", lambda e: e.memset(qb["thb"][:, 1:2], -TH2[5]),
          inc=None)


def emit_qdq(ph, sm, qb, src, sign_src, out_bf16, width, fold=1.0,
             out_guard=()):
    nb = width // 16
    V, Ssem, G = sm["sv"], sm["ss"], sm["sg"]
    fv, fv2 = qb["fv"], qb["fv2"]
    cA, cB, t1, t2, t3 = qb["cA"], qb["cB"], qb["t1"], qb["t2"], qb["t3"]
    sg45 = fv[:].bitcast(BF16)
    bm, sc, rec, rc2 = qb["bm"], qb["sc"], qb["rec"], qb["rc2"]
    src3 = src.rearrange("p (nb b) -> p nb b", b=16)
    ph.on("vector", lambda e: e.tensor_reduce(
        out=bm[:, 0:nb], in_=src3, axis=AXX, op=ALU.max,
        apply_absolute_value=True), waits=[(G, G.v)])
    ph.on("vector", lambda e: e.tensor_scalar(
        out=sc[:, 0:nb], in0=bm[:, 0:nb], scalar1=6e-12, scalar2=fold / 6.0,
        op0=ALU.max, op1=ALU.mult))
    ph.on("vector", lambda e: e.tensor_scalar(
        out=rc2[:, 0:nb], in0=bm[:, 0:nb], scalar1=6e-12, scalar2=1.0 / 6.0,
        op0=ALU.max, op1=ALU.mult))
    ph.on("vector", lambda e: e.reciprocal(out=rec[:, 0:nb], in_=rc2[:, 0:nb]))
    rec_b = _b16(rec, nb)
    sc_b = _b16(sc, nb)
    fvv = fv[:, 0:width]
    fv2v = fv2[:, 0:width]
    ph.on("vector", lambda e: e.tensor_tensor(
        out=fvv.rearrange("p (nb b) -> p nb b", b=16), in0=src3, in1=rec_b,
        op=ALU.mult), inc=V)
    v_ready = V.v
    ph.on("vector", lambda e: e.tensor_tensor(
        out=fv2v, in0=fvv, in1=fvv, op=ALU.mult), inc=V)
    v2_ready = V.v
    ph.on("scalar", lambda e: e.activation(
        out=qb["sgn"][:, 0:width], in_=sign_src, func=ACT.Sign),
        waits=[(V, v_ready), (G, G.v)], inc=Ssem)
    ph.on("scalar", lambda e: e.activation(
        out=sg45[:, 0:width], in_=fv2v, func=ACT.Sign,
        bias=qb["thb"][:, 0:1]), waits=[(V, v2_ready)], inc=Ssem)
    ph.on("scalar", lambda e: e.activation(
        out=sg45[:, width:2 * width], in_=fv2v, func=ACT.Sign,
        bias=qb["thb"][:, 1:2]), inc=Ssem)
    s_done = Ssem.v

    def cmp(dst, i):
        ph.on("vector", lambda e, i=i, dst=dst: e.tensor_scalar(
            out=dst[:, 0:width], in0=fv2v, scalar1=TH2[i], scalar2=WQ[i],
            op0=ALU.is_gt, op1=ALU.mult))

    def add(dst, a, b, w=()):
        ph.on("vector", lambda e, dst=dst, a=a, b=b: e.tensor_tensor(
            out=dst[:, 0:width], in0=a[:, 0:width], in1=b[:, 0:width],
            op=ALU.add), waits=w)

    cmp(cA, 0); cmp(cB, 1); add(t1, cA, cB)
    cmp(cA, 2); cmp(cB, 3); add(t2, cA, cB)
    add(t3, t1, t2)
    ph.on("vector", lambda e: e.tensor_tensor(
        out=t1[:, 0:width], in0=sg45[:, 0:width],
        in1=sg45[:, width:2 * width], op=ALU.add), waits=[(Ssem, s_done)])
    ph.on("vector", lambda e: e.tensor_scalar(
        out=t2[:, 0:width], in0=t1[:, 0:width], scalar1=0.5, scalar2=1.0,
        op0=ALU.mult, op1=ALU.add))
    cmp(cB, 6)
    add(t1, t2, cB)
    ph.on("vector", lambda e: e.tensor_tensor(
        out=cA[:, 0:width], in0=t3[:, 0:width], in1=t1[:, 0:width],
        op=ALU.add), inc=V)
    qa_done = V.v
    ph.on("gpsimd", lambda e: e.tensor_tensor(
        out=cB[:, 0:width], in0=cA[:, 0:width], in1=qb["sgn"][:, 0:width],
        op=ALU.mult), waits=[(V, qa_done), (Ssem, s_done)], inc=G)
    ph.on("gpsimd", lambda e: e.tensor_tensor(
        out=out_bf16.rearrange("p (nb b) -> p nb b", b=16),
        in0=cB[:, 0:width].rearrange("p (nb b) -> p nb b", b=16), in1=sc_b,
        op=ALU.mult), waits=list(out_guard), inc=G)
    return G.v


def emit_tile_stats(ph, sm, x2scr, sums_all, sqs_all, t, src, wait_src=()):
    V, Ssem = sm["sv"], sm["ss"]
    ph.on("scalar", lambda e: e.activation(
        out=x2scr[:, 0:D], in_=src, func=ACT.Square),
        waits=list(wait_src) + [(V, V.v)], inc=Ssem)
    x2_done = Ssem.v
    ph.on("vector", lambda e: e.tensor_reduce(
        out=sums_all[:, t:t + 1], in_=src, axis=AXX, op=ALU.add),
        waits=wait_src)
    ph.on("vector", lambda e: e.tensor_reduce(
        out=sqs_all[:, t:t + 1], in_=x2scr[:, 0:D], axis=AXX, op=ALU.add),
        waits=[(Ssem, x2_done)], inc=V)
    return V.v


def mk_fin_bufs(nc, es, prefix, nt):
    fb = {}
    for n in ("p2", "vt"):
        fb[n] = es.enter_context(nc.sbuf_tensor(prefix + n, [128, nt], F32))
    return fb


def emit_stats_final(ph, sm, fb, sums_all, sqs_all, m_all, veps_all, nt, wait_v):
    V = sm["sv"]
    ph.on("vector", lambda e: e.tensor_scalar(
        out=m_all[:, 0:nt], in0=sums_all[:, 0:nt], scalar1=1.0 / 1024.0,
        scalar2=None, op0=ALU.mult), waits=[(V, wait_v)])
    ph.on("vector", lambda e: e.tensor_tensor(
        out=fb["p2"][:, 0:nt], in0=m_all[:, 0:nt], in1=m_all[:, 0:nt],
        op=ALU.mult))
    ph.on("vector", lambda e: e.tensor_scalar(
        out=fb["vt"][:, 0:nt], in0=sqs_all[:, 0:nt], scalar1=1.0 / 1024.0,
        scalar2=EPS, op0=ALU.mult, op1=ALU.add))
    ph.on("vector", lambda e: e.tensor_tensor(
        out=veps_all[:, 0:nt], in0=fb["vt"][:, 0:nt], in1=fb["p2"][:, 0:nt],
        op=ALU.subtract), inc=V)
    return V.v


def emit_rstd(ph, sm, veps_all, rstd_all, scr, lo, n, wait_v):
    V, Ssem = sm["sv"], sm["ss"]
    ph.on("scalar", lambda e: e.activation(
        out=scr[:, lo:lo + n], in_=veps_all[:, lo:lo + n], func=ACT.Sqrt),
        waits=[(V, wait_v)], inc=Ssem)
    sq = Ssem.v
    ph.on("vector", lambda e: e.reciprocal(
        out=rstd_all[:, lo:lo + n], in_=scr[:, lo:lo + n]),
        waits=[(Ssem, sq)], inc=V)
    return V.v


def build_ffn_phase(nc, sm, src_dram, dst_dram, wgT, wuT, wdT, m_all, rstd_all,
                    identb, tag, out_stats=None, pre_waits=(), dbg=None):
    sldp, sstp, sv, ss, st_ = sm["sldp"], sm["sstp"], sm["sv"], sm["ss"], sm["st"]
    sg = sm["sg"]
    with ExitStack() as es:
        ph = Ph()
        for eng in ENGS:
            ph.wait(eng, pre_waits)
        sb = lambda n, shp, dt: es.enter_context(nc.sbuf_tensor(tag + n, shp, dt))
        xin = [sb("f_xin0", [128, D], F32), sb("f_xin1", [128, D], F32)]
        rout = [sb("f_rout0", [128, D], F32), sb("f_rout1", [128, D], F32)]
        x_ln = sb("f_xln", [128, D], F32)
        xq = [sb("f_xq0", [128, D], BF16)] * 2
        xqT = [sb("f_xqT0", [128, NDC, 128], BF16)] * 2
        qb = mk_qdq_bufs(nc, es, tag + "q_", IHALF)
        silu = [sb("f_silu0", [128, 512], F32), sb("f_silu1", [128, 512], F32)]
        hhalf = sb("f_hh", [128, IHALF], F32)
        hq = sb("f_hq", [128, IHALF], BF16)
        hqT = sb("f_hqT", [128, NH, 128], BF16)


        p_tr = es.enter_context(nc.psum_tensor(tag + "ptr", [128, 8, 128], BF16))
        p_tr2 = es.enter_context(nc.psum_tensor(tag + "ptr2", [128, 8, 128], BF16))
        p_g = [es.enter_context(nc.psum_tensor(f"{tag}pg{i}", [128, 512], F32))
               for i in range(2)]
        p_u = [es.enter_context(nc.psum_tensor(f"{tag}pu{i}", [128, 512], F32))
               for i in range(2)]
        p_d = [es.enter_context(nc.psum_tensor(f"{tag}pd{i}", [128, 512], F32))
               for i in range(2)]

        emit_qdq_init(ph, qb)
        slabs = [(0, 512), (512, 512), (1024, 384)]
        last_xin_free = [0, 0]
        last_store = [0, 0]
        last_xq_use = [0, 0]
        last_xqT_use = [0, 0]
        last_tr_evac = 0
        last_tr2_evac = 0
        last_gu_free = [[0, 0], [0, 0]]
        last_hqT_use = 0
        last_pd_free = 0
        gslab = 0

        for t in range(NT):
            p = t % 2
            ph.on("sync", lambda e, t=t, p=p: e.dma_start(
                out=xin[p][:], in_=src_dram[t * 128:(t + 1) * 128, :]),
                waits=[(sv, last_xin_free[p])], inc=sldp[p], inc_by=16)
            ld_here = sldp[p].v
            ph.on("vector", lambda e, t=t, p=p: e.tensor_scalar(
                out=x_ln[:], in0=xin[p][:], scalar1=m_all[:, t:t + 1],
                scalar2=rstd_all[:, t:t + 1], op0=ALU.subtract, op1=ALU.mult),
                waits=[(sldp[p], ld_here)], inc=sv)
            ph.wait("vector", [(st_, last_xq_use[p])])
            if dbg is not None and t == 0:
                xln_v = sv.v
                ph.on("gpsimd", lambda e: e.dma_start(
                    out=dbg["xln"][:], in_=x_ln[:]), waits=[(sv, xln_v)],
                    inc=sstp[0], inc_by=16)
                ph.wait("vector", [(sstp[0], sstp[0].v)])
            xq_ready = emit_qdq(ph, sm, qb, x_ln[:], x_ln[:], xq[p][:], D,
                                out_guard=[(st_, last_xq_use[p])])
            if dbg is not None and t == 0:
                ph.on("gpsimd", lambda e, p=p: e.dma_start(
                    out=dbg["xq"][:], in_=xq[p][:]), waits=[(sg, xq_ready)],
                    inc=sstp[0], inc_by=16)
                ph.wait("vector", [(sstp[0], sstp[0].v)])
            for k in range(NDC):
                ph.on("tensor", lambda e, k=k, p=p: e.transpose(
                    out=p_tr[:, k, :], in_=xq[p][:, k * 128:(k + 1) * 128],
                    identity=identb[:]),
                    waits=([(sg, xq_ready), (ss, last_tr_evac)] if k == 0 else ()),
                    inc=(st_ if k == NDC - 1 else None))
            last_xq_use[0] = last_xq_use[1] = st_.v
            trx_done = st_.v
            ph.on("scalar", lambda e, p=p: e.copy(out=xqT[p][:], in_=p_tr[:]),
                  waits=[(st_, trx_done), (st_, last_xqT_use[p])], inc=ss)
            last_tr_evac = ss.v
            xqT_ready = ss.v

            for half in range(2):
                ioff = half * IHALF
                for (soff, n) in slabs:
                    bp = gslab % 2
                    for k in range(NDC):
                        ph.on("tensor", lambda e, k=k, p=p, soff=soff, n=n, bp=bp, ioff=ioff: e.matmul(
                            out=p_g[bp][:, 0:n], lhsT=xqT[p][:, k, :],
                            rhs=wgT[:, k, ioff + soff:ioff + soff + n],
                            start=(k == 0), stop=(k == NDC - 1)),
                            waits=([(ss, xqT_ready), (sv, last_gu_free[0][bp])]
                                   if k == 0 else ()),
                            inc=(st_ if k == NDC - 1 else None))
                    g_done = st_.v
                    for k in range(NDC):
                        ph.on("tensor", lambda e, k=k, p=p, soff=soff, n=n, bp=bp, ioff=ioff: e.matmul(
                            out=p_u[bp][:, 0:n], lhsT=xqT[p][:, k, :],
                            rhs=wuT[:, k, ioff + soff:ioff + soff + n],
                            start=(k == 0), stop=(k == NDC - 1)),
                            waits=([(sv, last_gu_free[1][bp])] if k == 0 else ()),
                            inc=(st_ if k == NDC - 1 else None))
                    u_done = st_.v
                    ph.on("scalar", lambda e, bp=bp, n=n: e.activation(
                        out=silu[bp][:, 0:n], in_=p_g[bp][:, 0:n], func=ACT.Silu),
                        waits=[(st_, g_done)], inc=ss)
                    silu_done = ss.v
                    ph.on("vector", lambda e, bp=bp, n=n, soff=soff: e.tensor_tensor(
                        out=hhalf[:, soff:soff + n], in0=silu[bp][:, 0:n],
                        in1=p_u[bp][:, 0:n], op=ALU.mult),
                        waits=[(ss, silu_done), (st_, u_done)], inc=sv)
                    last_gu_free[0][bp] = sv.v
                    last_gu_free[1][bp] = sv.v
                    gslab += 1
                    if half == 1:
                        last_xqT_use[0] = last_xqT_use[1] = u_done
                ph.wait("vector", [(st_, last_hqT_use)])
                if dbg is not None and t == 0 and half == 0:
                    hh_v = sv.v
                    ph.on("gpsimd", lambda e: e.dma_start(
                        out=dbg["hh"][:], in_=hhalf[:]), waits=[(sv, hh_v)],
                        inc=sstp[0], inc_by=16)
                    ph.wait("vector", [(sstp[0], sstp[0].v)])
                hq_ready = emit_qdq(ph, sm, qb, hhalf[:], hhalf[:], hq[:],
                                    IHALF, out_guard=[(st_, last_hqT_use)])
                if dbg is not None and t == 0 and half == 0:
                    ph.on("gpsimd", lambda e: e.dma_start(
                        out=dbg["hq"][:], in_=hq[:]), waits=[(sg, hq_ready)],
                        inc=sstp[0], inc_by=16)
                    ph.wait("vector", [(sstp[0], sstp[0].v)])
                for j in range(NH):
                    tgt = p_tr[:, j, :] if j < 8 else p_tr2[:, j - 8, :]
                    w0 = [(sg, hq_ready), (ss, last_tr_evac)]
                    w8 = [(ss, last_tr2_evac)]
                    ph.on("tensor", lambda e, j=j, tgt=tgt: e.transpose(
                        out=tgt, in_=hq[:, j * 128:(j + 1) * 128],
                        identity=identb[:]),
                        waits=(w0 if j == 0 else (w8 if j == 8 else ())),
                        inc=(st_ if j == NH - 1 else None))
                trh_done = st_.v
                ph.on("scalar", lambda e: e.copy(
                    out=hqT[:, 0:8, :], in_=p_tr[:]),
                    waits=[(st_, trh_done), (st_, last_hqT_use)], inc=ss)
                last_tr_evac = ss.v
                ph.on("scalar", lambda e: e.copy(
                    out=hqT[:, 8:NH, :], in_=p_tr2[:, 0:3, :]), inc=ss)
                last_tr2_evac = ss.v
                hqT_ready = ss.v
                for dslab in range(2):
                    for j in range(NH):
                        w = []
                        if j == 0:
                            w = [(ss, hqT_ready)]
                            if half == 0 and dslab == 0:
                                w.append((sv, last_pd_free))
                        ph.on("tensor", lambda e, j=j, dslab=dslab, half=half: e.matmul(
                            out=p_d[dslab][:], lhsT=hqT[:, j, :],
                            rhs=wdT[:, half * NH + j,
                                    dslab * 512:(dslab + 1) * 512],
                            start=(half == 0 and j == 0),
                            stop=(half == 1 and j == NH - 1)),
                            waits=w,
                            inc=(st_ if (half == 1 and j == NH - 1) else None))
                last_hqT_use = st_.v
            down_done = st_.v
            ph.wait("vector", [(sstp[p], last_store[p])])
            for dslab in range(2):
                ph.on("vector", lambda e, dslab=dslab, p=p: e.tensor_tensor(
                    out=rout[p][:, dslab * 512:(dslab + 1) * 512],
                    in0=p_d[dslab][:],
                    in1=xin[p][:, dslab * 512:(dslab + 1) * 512], op=ALU.add),
                    waits=([(st_, down_done)] if dslab == 0 else ()), inc=sv)
            last_xin_free[p] = sv.v
            last_pd_free = sv.v
            resid_done = sv.v
            if out_stats is not None:
                o_sums, o_sqs = out_stats
                resid_done = emit_tile_stats(ph, sm, qb["fv2"], o_sums, o_sqs,
                                             t, rout[p][:],
                                             wait_src=[(sv, resid_done)])
            ph.on("scalar", lambda e, t=t, p=p: e.dma_start(
                out=dst_dram[t * 128:(t + 1) * 128, :], in_=rout[p][:]),
                waits=[(sv, resid_done)], inc=sstp[p], inc_by=16)
            last_store[p] = sstp[p].v
        ph.wait("scalar", [(sstp[0], sstp[0].v), (sstp[1], sstp[1].v)])
        ph.run(nc)


def build_program(debug=False):
    nc = bass.Bass("TRN2", target_bir_lowering=False)

    x_in = nc.dram_tensor("x_in", [TOK, D], F32, kind="ExternalInput").ap()
    wz2 = nc.dram_tensor("wz2", [D, D], F32, kind="ExternalInput").ap()
    wh2 = nc.dram_tensor("wh2", [D, D], F32, kind="ExternalInput").ap()
    wgd = nc.dram_tensor("wgd", [D, D], F32, kind="ExternalInput").ap()
    bzf = nc.dram_tensor("bzf", [D], F32, kind="ExternalInput").ap()
    bhf = nc.dram_tensor("bhf", [D], F32, kind="ExternalInput").ap()
    bgf = nc.dram_tensor("bgf", [D], F32, kind="ExternalInput").ap()
    wgate = nc.dram_tensor("wgate", [II, D], F32, kind="ExternalInput").ap()
    wup = nc.dram_tensor("wup", [II, D], F32, kind="ExternalInput").ap()
    wdown = nc.dram_tensor("wdown", [D, II], F32, kind="ExternalInput").ap()
    identb_d = nc.dram_tensor("identb_d", [128, 128], BF16, kind="ExternalInput").ap()
    identf_d = nc.dram_tensor("identf_d", [128, 128], F32, kind="ExternalInput").ap()
    out_d = nc.dram_tensor("out", [TOK, D], F32, kind="ExternalOutput").ap()
    kres = "ExternalOutput" if debug else "Internal"
    resid1 = nc.dram_tensor("resid1", [TOK, D], F32, kind=kres).ap()
    resid2 = nc.dram_tensor("resid2", [TOK, D], F32, kind=kres).ap()
    wg_st = nc.dram_tensor("wg_st", [NDC * 128, II], BF16, kind="Internal").ap()
    dbg = None
    if debug:
        dbg = {
            "xln": nc.dram_tensor("dbg_xln", [128, D], F32, kind="ExternalOutput").ap(),
            "xq": nc.dram_tensor("dbg_xq", [128, D], BF16, kind="ExternalOutput").ap(),
            "hh": nc.dram_tensor("dbg_hh", [128, IHALF], F32, kind="ExternalOutput").ap(),
            "hq": nc.dram_tensor("dbg_hq", [128, IHALF], BF16, kind="ExternalOutput").ap(),
            "wgT": nc.dram_tensor("dbg_wgT", [128, NDC, II], BF16, kind="ExternalOutput").ap(),
            "wdT": nc.dram_tensor("dbg_wdT", [128, NIC, D], BF16, kind="ExternalOutput").ap(),
            "mx": nc.dram_tensor("dbg_mx", [128, NT], F32, kind="ExternalOutput").ap(),
            "rstdx": nc.dram_tensor("dbg_rstdx", [128, NT], F32, kind="ExternalOutput").ap(),
            "wzb": nc.dram_tensor("dbg_wzb", [128, NDC, D], BF16, kind="ExternalOutput").ap(),
            "xlnT": nc.dram_tensor("dbg_xlnT", [128, NDC, S], BF16, kind="ExternalOutput").ap(),
            "abuf": nc.dram_tensor("dbg_abuf", [128, S], F32, kind="ExternalOutput").ap(),
            "bbuf": nc.dram_tensor("dbg_bbuf", [128, S], F32, kind="ExternalOutput").ap(),
            "h0": nc.dram_tensor("dbg_h0", [128, S], F32, kind="ExternalOutput").ap(),
            "hall": nc.dram_tensor("dbg_hall", [128, NDC, S], F32, kind="ExternalOutput").ap(),
            "rst1": nc.dram_tensor("dbg_rst1", [128, NT], F32, kind="ExternalOutput").ap(),
            "hb": nc.dram_tensor("dbg_hb", [128, NDC, 512], BF16, kind="ExternalOutput").ap(),
            "g": nc.dram_tensor("dbg_g", [128, NDC, 512], F32, kind="ExternalOutput").ap(),
            "ao": nc.dram_tensor("dbg_ao", [128, NDC, 512], F32, kind="ExternalOutput").ap(),
            "taoT": nc.dram_tensor("dbg_taoT", [128, 2, 4, 128], F32, kind="ExternalOutput").ap(),
            "wzb2": nc.dram_tensor("dbg_wzb2", [128, NDC, D], BF16, kind="ExternalOutput").ap(),
            "wgb2": nc.dram_tensor("dbg_wgb2", [128, NDC, D], BF16, kind="ExternalOutput").ap(),
            "pgt0": nc.dram_tensor("dbg_pgt0", [128, 512], F32, kind="ExternalOutput").ap(),
            "pgt1": nc.dram_tensor("dbg_pgt1", [128, 512], F32, kind="ExternalOutput").ap(),
            "wgb1": nc.dram_tensor("dbg_wgb1", [128, NDC, D], BF16, kind="ExternalOutput").ap(),
            "wgb15": nc.dram_tensor("dbg_wgb15", [128, NDC, D], BF16, kind="ExternalOutput").ap(),
            "bzs": nc.dram_tensor("dbg_bzs", [128, NDC], F32, kind="ExternalOutput").ap(),
            "st1": nc.dram_tensor("dbg_st1", [128, NT * 12], F32, kind="ExternalOutput").ap(),
            "m1": nc.dram_tensor("dbg_m1", [128, NT], F32, kind="ExternalOutput").ap(),
            "v1": nc.dram_tensor("dbg_v1", [128, NT], F32, kind="ExternalOutput").ap(),
            "fsq": nc.dram_tensor("dbg_fsq", [128, NT, 4], F32, kind="ExternalOutput").ap(),
            "fms": nc.dram_tensor("dbg_fms", [128, NT], F32, kind="ExternalOutput").ap(),
        }
    wu_st = nc.dram_tensor("wu_st", [NDC * 128, II], BF16, kind="Internal").ap()
    wd_st = nc.dram_tensor("wd_st", [NIC * 128, D], BF16, kind="Internal").ap()

    prog = ExitStack()
    with prog:
        sm = {}
        for n in ("sld0", "sld1", "sst0", "sst1", "smisc", "sv", "ss", "st", "sg"):
            sm[n] = Sem(prog.enter_context(nc.semaphore(n)), n)
        sldp = [sm["sld0"], sm["sld1"]]
        sstp = [sm["sst0"], sm["sst1"]]
        smisc = sm["smisc"]
        sm["sldp"] = sldp
        sm["sstp"] = sstp
        sv, ss, st_ = sm["sv"], sm["ss"], sm["st"]

        identb = prog.enter_context(nc.sbuf_tensor("identb_s", [128, 128], BF16))
        identf = prog.enter_context(nc.sbuf_tensor("identf_s", [128, 128], F32))
        m1_all = prog.enter_context(nc.sbuf_tensor("m1_all", [128, NT], F32))
        veps1_all = prog.enter_context(nc.sbuf_tensor("veps1_all", [128, NT], F32))
        rstd1_all = prog.enter_context(nc.sbuf_tensor("rstd1_all", [128, NT], F32))
        m2_all = prog.enter_context(nc.sbuf_tensor("m2_all", [128, NT], F32))
        veps2_all = prog.enter_context(nc.sbuf_tensor("veps2_all", [128, NT], F32))
        rstd2_all = prog.enter_context(nc.sbuf_tensor("rstd2_all", [128, NT], F32))
        mx_all = prog.enter_context(nc.sbuf_tensor("mx_all", [128, NT], F32))
        vepsx_all = prog.enter_context(nc.sbuf_tensor("vepsx_all", [128, NT], F32))
        rstdx_all = prog.enter_context(nc.sbuf_tensor("rstdx_all", [128, NT], F32))
        scr_all = prog.enter_context(nc.sbuf_tensor("scr_all", [128, NT], F32))
        sums1 = prog.enter_context(nc.sbuf_tensor("sums1", [128, NT], F32))
        sqs1 = prog.enter_context(nc.sbuf_tensor("sqs1", [128, NT], F32))
        sums2 = prog.enter_context(nc.sbuf_tensor("sums2", [128, NT], F32))
        sqs2 = prog.enter_context(nc.sbuf_tensor("sqs2", [128, NT], F32))

        with ExitStack() as es:
            ph = Ph()
            ph.on("sync", lambda e: e.dma_start(out=identb[:], in_=identb_d[:]),
                  inc=smisc, inc_by=16)
            ph.on("sync", lambda e: e.dma_start(out=identf[:], in_=identf_d[:]),
                  inc=smisc, inc_by=16)
            xa = [es.enter_context(nc.sbuf_tensor(f"pa_x{i}", [128, D], F32))
                  for i in range(2)]
            sumsx = es.enter_context(nc.sbuf_tensor("sumsx", [128, NT], F32))
            sqsx = es.enter_context(nc.sbuf_tensor("sqsx", [128, NT], F32))
            x2scr = es.enter_context(nc.sbuf_tensor("pa_x2", [128, D], F32))
            fb = mk_fin_bufs(nc, es, "pa_", NT)
            done_v = [0, 0]
            for t in range(NT):
                p = t % 2
                ph.on("sync", lambda e, t=t, p=p: e.dma_start(
                    out=xa[p][:], in_=x_in[t * 128:(t + 1) * 128, :]),
                    waits=[(sv, done_v[p])], inc=sldp[p], inc_by=16)
                done_v[p] = emit_tile_stats(ph, sm, x2scr, sumsx, sqsx, t,
                                            xa[p][:],
                                            wait_src=[(sldp[p], sldp[p].v)])
            fv_ = emit_stats_final(ph, sm, fb, sumsx, sqsx, mx_all, vepsx_all,
                                   NT, sv.v)
            rv = emit_rstd(ph, sm, vepsx_all, rstdx_all, scr_all, 0, NT, fv_)
            if dbg is not None:
                ph.on("gpsimd", lambda e: e.dma_start(
                    out=dbg["mx"][:], in_=mx_all[:]), waits=[(sv, rv)],
                    inc=sstp[0], inc_by=16)
                ph.on("gpsimd", lambda e: e.dma_start(
                    out=dbg["rstdx"][:], in_=rstdx_all[:]), inc=sstp[0], inc_by=16)
            ph.wait("gpsimd", [(sstp[0], sstp[0].v)])
            ph.run(nc)

        with ExitStack() as esw:
            wgT = esw.enter_context(nc.sbuf_tensor("wgT", [128, NDC, II], BF16))
            wuT = esw.enter_context(nc.sbuf_tensor("wuT", [128, NDC, II], BF16))
            wdT = esw.enter_context(nc.sbuf_tensor("wdT", [128, NIC, D], BF16))

            with ExitStack() as es:
                ph = Ph()
                ph.wait("tensor", [(smisc, 32)])
                wst = [es.enter_context(nc.sbuf_tensor(f"wp_st{i}", [128, II], F32))
                       for i in range(2)]
                qb = mk_qdq_bufs(nc, es, "wp_", IHALF)
                wq = es.enter_context(nc.sbuf_tensor("wp_wq", [128, II], BF16))
                p_tr = es.enter_context(nc.psum_tensor("wp_ptr", [128, 8, 128], BF16))
                p_tr2 = es.enter_context(nc.psum_tensor("wp_ptr2", [128, 8, 128], BF16))
                emit_qdq_init(ph, qb)
                state = {"tr": 0, "tr2": 0, "wv": [0, 0], "trT": 0}

                def prep_gu(wdram, wT):
                    for c in range(NIC):
                        p = c % 2
                        ph.on("sync", lambda e, c=c, p=p, wdram=wdram: e.dma_start(
                            out=wst[p][:, 0:D], in_=wdram[c * 128:(c + 1) * 128, :]),
                            waits=[(sm["sg"], state["wv"][p])],
                            inc=sldp[p], inc_by=16)
                        ph.wait("vector", [(sldp[p], sldp[p].v), (st_, state["trT"])])
                        q_ready = emit_qdq(ph, sm, qb, wst[p][:, 0:D],
                                           wst[p][:, 0:D], wq[:, 0:D], D,
                                           out_guard=[(st_, state["trT"])])
                        state["wv"][p] = q_ready
                        for k in range(NDC):
                            ph.on("tensor", lambda e, k=k: e.transpose(
                                out=p_tr[:, k, :],
                                in_=wq[:, k * 128:(k + 1) * 128],
                                identity=identb[:]),
                                waits=([(sm["sg"], q_ready), (ss, state["tr"])]
                                       if k == 0 else ()),
                                inc=(st_ if k == NDC - 1 else None))
                        tr_done = st_.v
                        state["trT"] = st_.v
                        ph.on("scalar", lambda e, c=c, wT=wT: e.copy(
                            out=wT[:, 0:NDC, c * 128:(c + 1) * 128], in_=p_tr[:]),
                            waits=[(st_, tr_done)], inc=ss)
                        state["tr"] = ss.v

                def prep_down():
                    for c in range(NDC):
                        p = c % 2
                        ph.on("sync", lambda e, c=c, p=p: e.dma_start(
                            out=wst[p][:], in_=wdown[c * 128:(c + 1) * 128, :]),
                            waits=[(sm["sg"], state["wv"][p])],
                            inc=sldp[p], inc_by=16)
                        ph.wait("vector", [(sldp[p], sldp[p].v), (st_, state["trT"])])
                        emit_qdq(ph, sm, qb, wst[p][:, 0:IHALF],
                                 wst[p][:, 0:IHALF], wq[:, 0:IHALF], IHALF,
                                 fold=0.5, out_guard=[(st_, state["trT"])])
                        q_ready = emit_qdq(ph, sm, qb, wst[p][:, IHALF:II],
                                           wst[p][:, IHALF:II], wq[:, IHALF:II],
                                           IHALF, fold=0.5,
                                           out_guard=[(st_, state["trT"])])
                        state["wv"][p] = q_ready
                        for grp, (j0, nj) in enumerate(((0, 8), (8, 8), (16, 6))):
                            ptr = p_tr if grp != 1 else p_tr2
                            trs = state["tr"] if grp != 1 else state["tr2"]
                            for j in range(nj):
                                ph.on("tensor", lambda e, j=j, j0=j0, ptr=ptr: e.transpose(
                                    out=ptr[:, j, :],
                                    in_=wq[:, (j0 + j) * 128:(j0 + j + 1) * 128],
                                    identity=identb[:]),
                                    waits=([(sm["sg"], q_ready), (ss, trs)]
                                           if j == 0 else ()),
                                    inc=(st_ if j == nj - 1 else None))
                            tr_done = st_.v
                            state["trT"] = st_.v
                            ph.on("scalar", lambda e, c=c, j0=j0, nj=nj, ptr=ptr: e.copy(
                                out=wdT[:, j0:j0 + nj, c * 128:(c + 1) * 128],
                                in_=ptr[:, 0:nj, :]),
                                waits=[(st_, tr_done)], inc=ss)
                            if grp != 1:
                                state["tr"] = ss.v
                            else:
                                state["tr2"] = ss.v

                prep_gu(wgate, wgT)
                prep_gu(wup, wuT)
                prep_down()
                wv = ss.v
                ph.on("gpsimd", lambda e: e.dma_start(
                    out=wg_st.rearrange("(k p) i -> p k i", p=128), in_=wgT[:]),
                    waits=[(ss, wv)], inc=smisc, inc_by=16)
                ph.on("gpsimd", lambda e: e.dma_start(
                    out=wu_st.rearrange("(k p) i -> p k i", p=128), in_=wuT[:]),
                    inc=smisc, inc_by=16)
                ph.on("gpsimd", lambda e: e.dma_start(
                    out=wd_st.rearrange("(k p) i -> p k i", p=128), in_=wdT[:]),
                    inc=smisc, inc_by=16)
                if dbg is not None:
                    wv2 = ss.v
                    ph.on("gpsimd", lambda e: e.dma_start(
                        out=dbg["wgT"][:], in_=wgT[:]), waits=[(ss, wv2)],
                        inc=smisc, inc_by=16)
                    ph.on("gpsimd", lambda e: e.dma_start(
                        out=dbg["wdT"][:], in_=wdT[:]), inc=smisc, inc_by=16)
                sm["stash_done"] = smisc.v
                ph.wait("gpsimd", [(smisc, smisc.v)])
                ph.run(nc)

            build_ffn_phase(nc, sm, x_in, resid1, wgT[:], wuT[:], wdT[:],
                            mx_all, rstdx_all, identb, "f1_",
                            out_stats=(sums1, sqs1), dbg=dbg)

        with ExitStack() as esa:
            wzb = esa.enter_context(nc.sbuf_tensor("wzb", [128, NDC, D], BF16))
            whb = esa.enter_context(nc.sbuf_tensor("whb", [128, NDC, D], BF16))
            wgb = esa.enter_context(nc.sbuf_tensor("wgb", [128, NDC, D], BF16))
            bz_s = esa.enter_context(nc.sbuf_tensor("bz_s", [128, NDC], F32))
            nbz_s = esa.enter_context(nc.sbuf_tensor("nbz_s", [128, NDC], F32))
            bh_s = esa.enter_context(nc.sbuf_tensor("bh_s", [128, NDC], F32))
            bg_s = esa.enter_context(nc.sbuf_tensor("bg_s", [128, NDC], F32))

            with ExitStack() as es:
                ph = Ph()
                wstg = [es.enter_context(nc.sbuf_tensor(f"aw{i}", [128, D], F32))
                        for i in range(2)]
                cnt = {"v": [0, 0]}
                for wi, (wsrc, wdst) in enumerate(((wz2, wzb), (wh2, whb), (wgd, wgb))):
                    for c in range(NDC):
                        p = (wi * NDC + c) % 2
                        ph.on("sync", lambda e, c=c, p=p, wsrc=wsrc: e.dma_start(
                            out=wstg[p][:], in_=wsrc[c * 128:(c + 1) * 128, :]),
                            waits=[(sv, cnt["v"][p])], inc=sldp[p], inc_by=16)
                        ph.on("vector", lambda e, c=c, p=p, wdst=wdst: e.tensor_copy(
                            out=wdst[:, c, :], in_=wstg[p][:]),
                            waits=[(sldp[p], sldp[p].v)], inc=sv)
                        cnt["v"][p] = sv.v
                for bsrc, bdst in ((bzf, bz_s), (bhf, bh_s), (bgf, bg_s)):
                    ph.on("sync", lambda e, bsrc=bsrc, bdst=bdst: e.dma_start(
                        out=bdst[:], in_=bsrc.rearrange("(c p) -> p c", p=128),
                        allow_slow_non_contiguous=True),
                        inc=smisc, inc_by=16)
                ph.on("vector", lambda e: e.tensor_scalar(
                    out=nbz_s[:], in0=bz_s[:], scalar1=-1.0, scalar2=None,
                    op0=ALU.mult), waits=[(smisc, smisc.v)], inc=sv)
                if dbg is not None:
                    wv3 = sv.v
                    ph.on("gpsimd", lambda e: e.dma_start(
                        out=dbg["st1"][:, 0:NT], in_=sums1[:]), waits=[(sv, wv3)],
                        inc=sstp[0], inc_by=16)
                    ph.on("gpsimd", lambda e: e.dma_start(
                        out=dbg["wzb"][:], in_=wzb[:]),
                        inc=sstp[0], inc_by=16)
                    ph.on("gpsimd", lambda e: e.dma_start(
                        out=dbg["wgb1"][:], in_=wgb[:]), inc=sstp[0], inc_by=16)
                    ph.on("gpsimd", lambda e: e.dma_start(
                        out=dbg["bzs"][:], in_=bz_s[:]), inc=sstp[0], inc_by=16)
                fb1 = mk_fin_bufs(nc, es, "d_", NT)
                fv1 = emit_stats_final(ph, sm, fb1, sums1, sqs1, m1_all,
                                       veps1_all, NT, sv.v)
                if dbg is not None:
                    ph.on("gpsimd", lambda e: e.dma_start(
                        out=dbg["m1"][:], in_=m1_all[:]), waits=[(sv, fv1)],
                        inc=sstp[0], inc_by=16)
                    ph.on("gpsimd", lambda e: e.dma_start(
                        out=dbg["v1"][:], in_=veps1_all[:]), inc=sstp[0], inc_by=16)
                    ph.on("gpsimd", lambda e: e.dma_start(
                        out=dbg["fms"][:], in_=fb1["p2"][:]), inc=sstp[0], inc_by=16)
                rrv = emit_rstd(ph, sm, veps1_all, rstd1_all, scr_all, 0, NT, fv1)
                if dbg is not None:
                    ph.on("gpsimd", lambda e: e.dma_start(
                        out=dbg["rst1"][:], in_=rstd1_all[:]), waits=[(sv, rrv)],
                        inc=sstp[0], inc_by=16)
                ph.wait("gpsimd", [(sstp[0], sstp[0].v)])
                ph.run(nc)

            for s in range(SEQ_PC):
                with ExitStack() as esq:
                    h_t = esq.enter_context(
                        nc.sbuf_tensor(f"a_h{s}", [128, NDC, S], F32))
                    with ExitStack() as es:
                        ph = Ph()
                        xlnT = es.enter_context(
                            nc.sbuf_tensor(f"a_xlnT{s}", [128, NDC, S], BF16))
                        xl_bf = es.enter_context(
                            nc.sbuf_tensor(f"a_xlbf{s}", [128, D], BF16))
                        rtile = [es.enter_context(
                            nc.sbuf_tensor(f"a_rt{s}_{i}", [128, D], F32))
                            for i in range(2)]
                        a_buf = es.enter_context(
                            nc.sbuf_tensor(f"a_a{s}", [128, S], F32))
                        bb_buf = es.enter_context(
                            nc.sbuf_tensor(f"a_bb{s}", [128, S], F32))
                        z_sb = [es.enter_context(
                            nc.sbuf_tensor(f"a_z{s}_{i}", [128, 512], F32))
                            for i in range(2)]
                        p_tra = es.enter_context(
                            nc.psum_tensor(f"a_ptra{s}", [128, 8, 128], BF16))
                        p_z = [es.enter_context(
                            nc.psum_tensor(f"a_pz{s}_{i}", [128, 512], F32))
                            for i in range(2)]
                        p_ht = [es.enter_context(
                            nc.psum_tensor(f"a_pht{s}_{i}", [128, 512], F32))
                            for i in range(2)]
                        lva = [0, 0]
                        evA = 0
                        vbf = 0
                        for q in range(TPS):
                            g = s * TPS + q
                            p = q % 2
                            ph.on("sync", lambda e, g=g, p=p: e.dma_start(
                                out=rtile[p][:],
                                in_=resid1[g * 128:(g + 1) * 128, :]),
                                waits=[(sv, lva[p])],
                                inc=sldp[p], inc_by=16)
                            ph.on("vector", lambda e, g=g, p=p: e.tensor_scalar(
                                out=xl_bf[:], in0=rtile[p][:],
                                scalar1=m1_all[:, g:g + 1],
                                scalar2=rstd1_all[:, g:g + 1],
                                op0=ALU.subtract, op1=ALU.mult),
                                waits=[(sldp[p], sldp[p].v), (st_, vbf)], inc=sv)
                            lva[p] = sv.v
                            xl_ready = sv.v
                            for k in range(NDC):
                                ph.on("tensor", lambda e, k=k: e.transpose(
                                    out=p_tra[:, k, :],
                                    in_=xl_bf[:, k * 128:(k + 1) * 128],
                                    identity=identb[:]),
                                    waits=([(sv, xl_ready), (ss, evA)]
                                           if k == 0 else ()),
                                    inc=(st_ if k == NDC - 1 else None))
                            vbf = st_.v
                            ph.on("scalar", lambda e, q=q: e.copy(
                                out=xlnT[:, 0:NDC, q * 128:(q + 1) * 128],
                                in_=p_tra[:]), waits=[(st_, st_.v)], inc=ss)
                            evA = ss.v
                        xlnT_ready = ss.v
                        if dbg is not None and s == DBG_SEQ:
                            ph.on("gpsimd", lambda e: e.dma_start(
                                out=dbg["xlnT"][:], in_=xlnT[:]),
                                waits=[(ss, xlnT_ready)], inc=sstp[0], inc_by=16)
                        zfree = [0, 0]
                        zuse = [0, 0]
                        htfree = [0, 0]
                        last_scan = 0
                        for c in range(NDC):
                            sa_done = 0
                            for sl in range(4):
                                bp = sl % 2
                                for k in range(NDC):
                                    ph.on("tensor", lambda e, k=k, c=c, sl=sl, bp=bp: e.matmul(
                                        out=p_z[bp][:], lhsT=wzb[:, k, c * 128:(c + 1) * 128],
                                        rhs=xlnT[:, k, sl * 512:(sl + 1) * 512],
                                        start=(k == 0), stop=(k == NDC - 1)),
                                        waits=([(ss, xlnT_ready), (ss, zfree[bp])]
                                               if k == 0 else ()),
                                        inc=(st_ if k == NDC - 1 else None))
                                z_done = st_.v
                                for k in range(NDC):
                                    ph.on("tensor", lambda e, k=k, c=c, sl=sl, bp=bp: e.matmul(
                                        out=p_ht[bp][:], lhsT=whb[:, k, c * 128:(c + 1) * 128],
                                        rhs=xlnT[:, k, sl * 512:(sl + 1) * 512],
                                        start=(k == 0), stop=(k == NDC - 1)),
                                        waits=([(sv, htfree[bp])] if k == 0 else ()),
                                        inc=(st_ if k == NDC - 1 else None))
                                ht_done = st_.v
                                ph.on("scalar", lambda e, bp=bp, c=c: e.activation(
                                    out=z_sb[bp][:], in_=p_z[bp][:],
                                    func=ACT.Sigmoid, bias=bz_s[:, c:c + 1]),
                                    waits=[(st_, z_done), (sv, zuse[bp])], inc=ss)
                                zs_done = ss.v
                                ph.on("scalar", lambda e, bp=bp, c=c, sl=sl: e.activation(
                                    out=a_buf[:, sl * 512:(sl + 1) * 512],
                                    in_=p_z[bp][:], func=ACT.Sigmoid,
                                    bias=nbz_s[:, c:c + 1], scale=-1.0),
                                    waits=[(sv, last_scan)], inc=ss)
                                zfree[bp] = ss.v
                                sa_done = ss.v
                                ph.on("vector", lambda e, bp=bp, c=c, sl=sl: e.scalar_tensor_tensor(
                                    out=bb_buf[:, sl * 512:(sl + 1) * 512],
                                    in0=p_ht[bp][:], scalar=bh_s[:, c:c + 1],
                                    in1=z_sb[bp][:], op0=ALU.add, op1=ALU.mult),
                                    waits=[(ss, zs_done), (st_, ht_done)], inc=sv)
                                htfree[bp] = sv.v
                                zuse[bp] = sv.v
                            if dbg is not None and s == DBG_SEQ and c == 0:
                                av = sa_done
                                bv = sv.v
                                ph.on("gpsimd", lambda e: e.dma_start(
                                    out=dbg["abuf"][:], in_=a_buf[:]),
                                    waits=[(ss, av), (sv, bv)], inc=sstp[0], inc_by=16)
                                ph.on("gpsimd", lambda e: e.dma_start(
                                    out=dbg["bbuf"][:], in_=bb_buf[:]),
                                    inc=sstp[0], inc_by=16)
                                ph.wait("vector", [(sstp[0], sstp[0].v)])
                            ph.on("vector", lambda e, c=c: e.tensor_tensor_scan(
                                out=h_t[:, c, :], data0=a_buf[:], data1=bb_buf[:],
                                initial=0.0, op0=ALU.mult, op1=ALU.add),
                                waits=[(ss, sa_done)], inc=sv)
                            last_scan = sv.v
                            if dbg is not None and s == DBG_SEQ and c == NDC - 1:
                                ph.on("gpsimd", lambda e: e.dma_start(
                                    out=dbg["wgb15"][:], in_=wgb[:]),
                                    waits=[(sv, sv.v)], inc=sstp[0], inc_by=16)
                            if dbg is not None and s == DBG_SEQ and c == 0:
                                ph.on("gpsimd", lambda e: e.dma_start(
                                    out=dbg["h0"][:], in_=h_t[:, 0, :]),
                                    waits=[(sv, sv.v)], inc=sstp[0], inc_by=16)
                        if dbg is not None and s == DBG_SEQ:
                            ph.on("gpsimd", lambda e: e.dma_start(
                                out=dbg["hall"][:], in_=h_t[:]),
                                waits=[(sv, sv.v)], inc=sstp[0], inc_by=16)
                        ph.wait("gpsimd", [(sstp[0], sstp[0].v)])
                        ph.run(nc)

                    with ExitStack() as es:
                        ph = Ph()
                        hb_sl = es.enter_context(
                            nc.sbuf_tensor(f"c_hb{s}", [128, NDC, 512], BF16))
                        g_sb = [es.enter_context(
                            nc.sbuf_tensor(f"c_g{s}_{i}", [128, 512], F32))
                            for i in range(2)]
                        ao = es.enter_context(
                            nc.sbuf_tensor(f"c_ao{s}", [128, NDC, 512], F32))
                        rtile = [es.enter_context(
                            nc.sbuf_tensor(f"c_rt{s}_{i}", [128, D], F32))
                            for i in range(2)]
                        rout2 = [es.enter_context(
                            nc.sbuf_tensor(f"c_ro{s}_{i}", [128, D], F32))
                            for i in range(2)]
                        x2c = es.enter_context(
                            nc.sbuf_tensor(f"c_x2_{s}", [128, D], F32))
                        p_gt = [es.enter_context(
                            nc.psum_tensor(f"c_pgt{s}_{i}", [128, 512], F32))
                            for i in range(2)]
                        p_tao = [es.enter_context(
                            nc.psum_tensor(f"c_ptao{s}_{i}", [128, 4, 128], F32))
                            for i in range(2)]
                        scan_all = sv.v
                        gfree = [0, 0]
                        hb_free = 0
                        ao_free = 0
                        tao_free = [0, 0]
                        lvc = [0, 0]
                        stc = [0, 0]
                        for sl in range(4):
                            for k in range(NDC):
                                ph.on("scalar", lambda e, k=k, sl=sl: e.copy(
                                    out=hb_sl[:, k, :],
                                    in_=h_t[:, k, sl * 512:(sl + 1) * 512]),
                                    waits=([(sv, scan_all), (st_, hb_free)]
                                           if k == 0 else ()), inc=ss)
                            hb_ready = ss.v
                            if dbg is not None and s == DBG_SEQ and sl == 0:
                                ph.on("gpsimd", lambda e: e.dma_start(
                                    out=dbg["hb"][:], in_=hb_sl[:]),
                                    waits=[(ss, hb_ready)], inc=sstp[0], inc_by=16)
                            for m in range(NDC):
                                bp = m % 2
                                for k in range(NDC):
                                    ph.on("tensor", lambda e, k=k, m=m, bp=bp: e.matmul(
                                        out=p_gt[bp][:], lhsT=wgb[:, k, m * 128:(m + 1) * 128],
                                        rhs=hb_sl[:, k, :],
                                        start=(k == 0), stop=(k == NDC - 1)),
                                        waits=([(ss, hb_ready), (sv, gfree[bp])]
                                               if k == 0 else ()),
                                        inc=(st_ if k == NDC - 1 else None))
                                gt_done = st_.v
                                if m == NDC - 1:
                                    hb_free = st_.v
                                ph.on("scalar", lambda e, bp=bp, m=m: e.activation(
                                    out=g_sb[bp][:], in_=p_gt[bp][:],
                                    func=ACT.Sigmoid, bias=bg_s[:, m:m + 1]),
                                    waits=[(st_, gt_done)], inc=ss)
                                gs_done = ss.v
                                ph.on("vector", lambda e, bp=bp, m=m, sl=sl: e.tensor_tensor(
                                    out=ao[:, m, :], in0=g_sb[bp][:],
                                    in1=h_t[:, m, sl * 512:(sl + 1) * 512],
                                    op=ALU.mult),
                                    waits=[(ss, gs_done), (st_, ao_free)], inc=sv)
                                gfree[bp] = sv.v
                                if dbg is not None and s == DBG_SEQ and sl == 0:
                                    ph.on("gpsimd", lambda e, m=m: e.dma_start(
                                        out=dbg["g"][:, m, :], in_=g_sb[bp][:]),
                                        waits=[(ss, gs_done)], inc=sstp[0], inc_by=16)
                                    ph.wait("scalar", [(sstp[0], sstp[0].v)])
                            ao_ready = sv.v
                            if dbg is not None and s == DBG_SEQ and sl == 0:
                                ph.on("gpsimd", lambda e: e.dma_start(
                                    out=dbg["ao"][:], in_=ao[:]),
                                    waits=[(sv, ao_ready)], inc=sstp[0], inc_by=16)
                            for q in range(4):
                                g = s * TPS + sl * 4 + q
                                p = q % 2
                                ph.on("sync", lambda e, g=g, p=p: e.dma_start(
                                    out=rtile[p][:],
                                    in_=resid1[g * 128:(g + 1) * 128, :]),
                                    waits=[(sv, lvc[p])], inc=sldp[p], inc_by=16)
                                ld_here = sldp[p].v
                                for m in range(NDC):
                                    ph.on("tensor", lambda e, m=m, q=q: e.matmul(
                                        out=p_tao[m // 4][:, m % 4, :],
                                        lhsT=ao[:, m, q * 128:(q + 1) * 128],
                                        rhs=identf[:], start=True, stop=True),
                                        waits=([(sv, ao_ready),
                                                (sv, tao_free[0]),
                                                (sv, tao_free[1])]
                                               if m == 0 else ()),
                                        inc=(st_ if m == NDC - 1 else None))
                                if q == 3:
                                    ao_free = st_.v
                                tao_done = st_.v
                                if dbg is not None and s == DBG_SEQ and sl == 0 and q == 0:
                                    ph.on("vector", lambda e: e.tensor_copy(
                                        out=rout2[1][:, 0:512],
                                        in_=p_tao[0][:].rearrange("p a b -> p (a b)")),
                                        waits=[(st_, tao_done)], inc=sv)
                                    ph.on("vector", lambda e: e.tensor_copy(
                                        out=rout2[1][:, 512:1024],
                                        in_=p_tao[1][:].rearrange("p a b -> p (a b)")), inc=sv)
                                    ph.on("gpsimd", lambda e: e.dma_start(
                                        out=dbg["taoT"][:].rearrange("p a b c -> p (a b c)"),
                                        in_=rout2[1][:]),
                                        waits=[(sv, sv.v)], inc=sstp[0], inc_by=16)
                                    ph.wait("vector", [(sstp[0], sstp[0].v)])
                                ph.wait("vector", [(sstp[p], stc[p])])
                                for half2 in range(2):
                                    ph.on("vector", lambda e, half2=half2, p=p: e.tensor_tensor(
                                        out=rout2[p][:, half2 * 512:(half2 + 1) * 512],
                                        in0=p_tao[half2][:].rearrange("p a b -> p (a b)"),
                                        in1=rtile[p][:, half2 * 512:(half2 + 1) * 512],
                                        op=ALU.add),
                                        waits=([(st_, tao_done), (sldp[p], ld_here)]
                                               if half2 == 0 else ()), inc=sv)
                                lvc[p] = sv.v
                                tao_free[0] = sv.v
                                tao_free[1] = sv.v
                                rdone = emit_tile_stats(
                                    ph, sm, x2c, sums2, sqs2, g, rout2[p][:],
                                    wait_src=[(sv, sv.v)])
                                ph.on("scalar", lambda e, g=g, p=p: e.dma_start(
                                    out=resid2[g * 128:(g + 1) * 128, :],
                                    in_=rout2[p][:]),
                                    waits=[(sv, rdone)], inc=sstp[p], inc_by=16)
                                stc[p] = sstp[p].v
                        if dbg is not None and s == SEQ_PC - 1:
                            ph.on("gpsimd", lambda e: e.dma_start(
                                out=dbg["wzb2"][:], in_=wzb[:]), inc=sstp[0], inc_by=16)
                            ph.on("gpsimd", lambda e: e.dma_start(
                                out=dbg["wgb2"][:], in_=wgb[:]), inc=sstp[0], inc_by=16)
                        ph.wait("scalar", [(sstp[0], sstp[0].v), (sstp[1], sstp[1].v)])
                        ph.run(nc)

        with ExitStack() as esw:
            wgT = esw.enter_context(nc.sbuf_tensor("wgT2", [128, NDC, II], BF16))
            wuT = esw.enter_context(nc.sbuf_tensor("wuT2", [128, NDC, II], BF16))
            wdT = esw.enter_context(nc.sbuf_tensor("wdT2", [128, NIC, D], BF16))
            with ExitStack() as es:
                ph = Ph()
                ph.wait("sync", [(smisc, sm["stash_done"])])
                ph.on("sync", lambda e: e.dma_start(
                    out=wgT[:], in_=wg_st.rearrange("(k p) i -> p k i", p=128)),
                    inc=smisc, inc_by=16)
                ph.on("sync", lambda e: e.dma_start(
                    out=wuT[:], in_=wu_st.rearrange("(k p) i -> p k i", p=128)),
                    inc=smisc, inc_by=16)
                ph.on("sync", lambda e: e.dma_start(
                    out=wdT[:], in_=wd_st.rearrange("(k p) i -> p k i", p=128)),
                    inc=smisc, inc_by=16)
                fb2 = mk_fin_bufs(nc, es, "g_", NT)
                fv2 = emit_stats_final(ph, sm, fb2, sums2, sqs2, m2_all,
                                       veps2_all, NT, sv.v)
                emit_rstd(ph, sm, veps2_all, rstd2_all, scr_all, 0, NT, fv2)
                ph.wait("tensor", [(smisc, smisc.v)])
                ph.run(nc)
            build_ffn_phase(nc, sm, resid2, out_d, wgT[:], wuT[:], wdT[:],
                            m2_all, rstd2_all, identb, "f3_", out_stats=None)

        with ExitStack() as es:
            ph = Ph()
            ph.wait("gpsimd", [(sstp[0], sstp[0].v), (sstp[1], sstp[1].v),
                               (smisc, smisc.v)])
            ph.run(nc)

    return nc


_NC_CACHE = {}


def _get_nc(debug=False):
    if debug not in _NC_CACHE:
        _NC_CACHE[debug] = build_program(debug)
    return _NC_CACHE[debug]


def make_in_maps(L0_attn_x, L0_attn_Wz, L0_attn_bz, L0_attn_Wh, L0_attn_bh,
                 L0_attn_Wg, L0_attn_bg, L0_ffn_w_gate, L0_ffn_w_up,
                 L0_ffn_w_down):
    x = np.ascontiguousarray(np.asarray(L0_attn_x, dtype=np.float32))
    wz2 = np.ascontiguousarray(
        np.transpose(np.asarray(L0_attn_Wz, np.float32), (1, 0, 2)).reshape(D, D))
    wh2 = np.ascontiguousarray(
        np.transpose(np.asarray(L0_attn_Wh, np.float32), (1, 0, 2)).reshape(D, D))
    wgd = np.ascontiguousarray(np.asarray(L0_attn_Wg, np.float32))
    bzf = np.ascontiguousarray(np.asarray(L0_attn_bz, np.float32).reshape(D))
    bhf = np.ascontiguousarray(np.asarray(L0_attn_bh, np.float32).reshape(D))
    bgf = np.ascontiguousarray(np.asarray(L0_attn_bg, np.float32).reshape(D))
    wgate = np.ascontiguousarray(np.asarray(L0_ffn_w_gate, np.float32))
    wup = np.ascontiguousarray(np.asarray(L0_ffn_w_up, np.float32))
    wdown = np.ascontiguousarray(np.asarray(L0_ffn_w_down, np.float32))
    identb = np.eye(128, dtype=ml_dtypes.bfloat16)
    identf = np.eye(128, dtype=np.float32)
    shards = x.reshape(NCORES, TOK, D)
    return [{"x_in": np.ascontiguousarray(shards[i]), "wz2": wz2, "wh2": wh2,
             "wgd": wgd, "bzf": bzf, "bhf": bhf, "bgf": bgf, "wgate": wgate,
             "wup": wup, "wdown": wdown, "identb_d": identb,
             "identf_d": identf} for i in range(NCORES)]


def kernel(**inputs):
    nc = _get_nc(False)
    in_maps = make_in_maps(**inputs)
    res = run_bass_kernel_spmd(nc, in_maps, core_ids=list(range(NCORES)))
    out = np.stack([np.asarray(res.results[i]["out"]) for i in range(NCORES)], 0)
    return out.reshape(B, S, D).astype(np.float32)


# version 12
# speedup vs baseline: 1.0996x; 1.0996x over previous
import numpy as np
import ml_dtypes
from contextlib import ExitStack

import concourse.bass as bass
import concourse.mybir as mybir
from concourse.bass_utils import run_bass_kernel_spmd

F32 = mybir.dt.float32
BF16 = mybir.dt.bfloat16
ALU = mybir.AluOpType
ACT = mybir.ActivationFunctionType
AXX = mybir.AxisListType.X

B, S, D = 32, 2048, 1024
NCORES = 8
SEQ_PC = B // NCORES
TOK = SEQ_PC * S
NT = TOK // 128
TPS = S // 128
II = 2816
NIC = II // 128
NDC = D // 128
IHALF = II // 2
NH = IHALF // 128
EPS = 1e-5

_TH = (0.25, 0.75, 1.25, 1.75, 2.5, 3.5, 5.0)
TH2 = tuple(t * t for t in _TH)
WQ = (0.5, 0.5, 0.5, 0.5, 1.0, 1.0, 2.0)

ENGS = ("sync", "tensor", "vector", "scalar", "gpsimd")
import os as _os
DBG_SEQ = int(_os.environ.get("KDBG_SEQ", "0"))


class Sem:
    __slots__ = ("h", "v", "name")

    def __init__(self, h, name):
        self.h = h
        self.v = 0
        self.name = name


class Ph:

    def __init__(self):
        self.ops = {e: [] for e in ENGS}

    def on(self, eng, emit, waits=(), inc=None, inc_by=1):
        if inc is not None:
            inc.v += inc_by

        def f(e, emit=emit, waits=tuple(waits), inc=inc, inc_by=inc_by):
            for (s, v) in waits:
                if v > 0:
                    e.wait_ge(s.h, v)
            ins = emit(e)
            if inc is not None:
                ins.then_inc(inc.h, inc_by)
        self.ops[eng].append(f)

    def wait(self, eng, waits):
        def f(e, waits=tuple(waits)):
            for (s, v) in waits:
                if v > 0:
                    e.wait_ge(s.h, v)
        self.ops[eng].append(f)

    def run(self, nc):
        with nc.Block() as block:
            @block.sync
            def _(e):
                for f in self.ops["sync"]:
                    f(e)

            @block.tensor
            def _(e):
                for f in self.ops["tensor"]:
                    f(e)

            @block.vector
            def _(e):
                for f in self.ops["vector"]:
                    f(e)

            @block.scalar
            def _(e):
                for f in self.ops["scalar"]:
                    f(e)

            @block.gpsimd
            def _(e):
                for f in self.ops["gpsimd"]:
                    f(e)


def _b16(t, nb):
    return t[:, 0:nb].unsqueeze(2).broadcast_to([128, nb, 16])


def mk_qdq_bufs(nc, es, prefix, width):
    qb = {}
    for n in ("fv", "fv2"):
        qb[n] = es.enter_context(nc.sbuf_tensor(prefix + n, [128, width], F32))
    for n in ("cA", "cB", "t1", "t2", "t3", "sgn"):
        qb[n] = es.enter_context(nc.sbuf_tensor(prefix + n, [128, width], BF16))
    qb["thb"] = es.enter_context(
        nc.sbuf_tensor(prefix + "thb", [128, 4], F32))
    for n in ("bm", "sc", "rec", "rc2"):
        qb[n] = es.enter_context(nc.sbuf_tensor(prefix + n, [128, width // 16], F32))
    return qb


def emit_qdq_init(ph, qb):
    ph.on("vector", lambda e: e.memset(qb["thb"][:, 0:1], -TH2[4]))
    ph.on("vector", lambda e: e.memset(qb["thb"][:, 1:2], -TH2[5]))
    ph.on("vector", lambda e: e.memset(qb["thb"][:, 2:3], -TH2[2]))
    ph.on("vector", lambda e: e.memset(qb["thb"][:, 3:4], -TH2[3]),
          inc=None)


def emit_qdq(ph, sm, qb, src, sign_src, out_bf16, width, fold=1.0):
    nb = width // 16
    V, Ssem = sm["sv"], sm["ss"]
    fv, fv2 = qb["fv"], qb["fv2"]
    cA, cB, t1, t2, t3 = qb["cA"], qb["cB"], qb["t1"], qb["t2"], qb["t3"]
    sg45 = fv[:].bitcast(BF16)
    bm, sc, rec, rc2 = qb["bm"], qb["sc"], qb["rec"], qb["rc2"]
    src3 = src.rearrange("p (nb b) -> p nb b", b=16)
    ph.on("vector", lambda e: e.tensor_reduce(
        out=bm[:, 0:nb], in_=src3, axis=AXX, op=ALU.max,
        apply_absolute_value=True))
    ph.on("vector", lambda e: e.tensor_scalar(
        out=sc[:, 0:nb], in0=bm[:, 0:nb], scalar1=6e-12, scalar2=fold / 6.0,
        op0=ALU.max, op1=ALU.mult))
    ph.on("vector", lambda e: e.tensor_scalar(
        out=rc2[:, 0:nb], in0=bm[:, 0:nb], scalar1=6e-12, scalar2=1.0 / 6.0,
        op0=ALU.max, op1=ALU.mult))
    ph.on("vector", lambda e: e.reciprocal(out=rec[:, 0:nb], in_=rc2[:, 0:nb]))
    rec_b = _b16(rec, nb)
    sc_b = _b16(sc, nb)
    fvv = fv[:, 0:width]
    fv2v = fv2[:, 0:width]
    ph.on("vector", lambda e: e.tensor_tensor(
        out=fvv.rearrange("p (nb b) -> p nb b", b=16), in0=src3, in1=rec_b,
        op=ALU.mult), inc=V)
    v_ready = V.v
    ph.on("vector", lambda e: e.tensor_tensor(
        out=fv2v, in0=fvv, in1=fvv, op=ALU.mult), inc=V)
    v2_ready = V.v
    ph.on("scalar", lambda e: e.activation(
        out=qb["sgn"][:, 0:width], in_=sign_src, func=ACT.Sign),
        waits=[(V, v_ready)], inc=Ssem)
    ph.on("scalar", lambda e: e.activation(
        out=t2[:, 0:width], in_=fv2v, func=ACT.Sign,
        bias=qb["thb"][:, 2:3]), waits=[(V, v2_ready)], inc=Ssem)
    ph.on("scalar", lambda e: e.activation(
        out=t3[:, 0:width], in_=fv2v, func=ACT.Sign,
        bias=qb["thb"][:, 3:4]), inc=Ssem)
    s23_done = Ssem.v
    ph.on("scalar", lambda e: e.activation(
        out=sg45[:, 0:width], in_=fv2v, func=ACT.Sign,
        bias=qb["thb"][:, 0:1]), inc=Ssem)
    ph.on("scalar", lambda e: e.activation(
        out=sg45[:, width:2 * width], in_=fv2v, func=ACT.Sign,
        bias=qb["thb"][:, 1:2]), inc=Ssem)
    s_done = Ssem.v

    def cmp(dst, i):
        ph.on("vector", lambda e, i=i, dst=dst: e.tensor_scalar(
            out=dst[:, 0:width], in0=fv2v, scalar1=TH2[i], scalar2=WQ[i],
            op0=ALU.is_gt, op1=ALU.mult))

    def add(dst, a, b, w=()):
        ph.on("vector", lambda e, dst=dst, a=a, b=b: e.tensor_tensor(
            out=dst[:, 0:width], in0=a[:, 0:width], in1=b[:, 0:width],
            op=ALU.add), waits=w)

    cmp(cA, 0); cmp(cB, 1); add(t1, cA, cB)
    add(cA, t2, t3, w=[(Ssem, s23_done)])
    ph.on("vector", lambda e: e.scalar_tensor_tensor(
        out=cB[:, 0:width], in0=cA[:, 0:width], scalar=0.25,
        in1=t1[:, 0:width], op0=ALU.mult, op1=ALU.add))
    add(t1, sg45[:, 0:width], sg45[:, width:2 * width],
        w=[(Ssem, s_done)])
    ph.on("vector", lambda e: e.tensor_scalar(
        out=cA[:, 0:width], in0=t1[:, 0:width], scalar1=0.5, scalar2=1.5,
        op0=ALU.mult, op1=ALU.add))
    cmp(t2, 6)
    add(t3, cA, t2)
    add(cA, cB, t3)
    ph.on("vector", lambda e: e.tensor_tensor(
        out=cB[:, 0:width], in0=cA[:, 0:width], in1=qb["sgn"][:, 0:width],
        op=ALU.mult))
    ph.on("vector", lambda e: e.tensor_tensor(
        out=out_bf16.rearrange("p (nb b) -> p nb b", b=16),
        in0=cB[:, 0:width].rearrange("p (nb b) -> p nb b", b=16), in1=sc_b,
        op=ALU.mult), inc=V)
    return V.v


def emit_tile_stats(ph, sm, x2scr, sums_all, sqs_all, t, src, wait_src=()):
    V, Ssem = sm["sv"], sm["ss"]
    ph.on("scalar", lambda e: e.activation(
        out=x2scr[:, 0:D], in_=src, func=ACT.Square),
        waits=list(wait_src) + [(V, V.v)], inc=Ssem)
    x2_done = Ssem.v
    ph.on("vector", lambda e: e.tensor_reduce(
        out=sums_all[:, t:t + 1], in_=src, axis=AXX, op=ALU.add),
        waits=wait_src)
    ph.on("vector", lambda e: e.tensor_reduce(
        out=sqs_all[:, t:t + 1], in_=x2scr[:, 0:D], axis=AXX, op=ALU.add),
        waits=[(Ssem, x2_done)], inc=V)
    return V.v


def mk_fin_bufs(nc, es, prefix, nt):
    fb = {}
    for n in ("p2", "vt"):
        fb[n] = es.enter_context(nc.sbuf_tensor(prefix + n, [128, nt], F32))
    return fb


def emit_stats_final(ph, sm, fb, sums_all, sqs_all, m_all, veps_all, nt, wait_v):
    V = sm["sv"]
    ph.on("vector", lambda e: e.tensor_scalar(
        out=m_all[:, 0:nt], in0=sums_all[:, 0:nt], scalar1=1.0 / 1024.0,
        scalar2=None, op0=ALU.mult), waits=[(V, wait_v)])
    ph.on("vector", lambda e: e.tensor_tensor(
        out=fb["p2"][:, 0:nt], in0=m_all[:, 0:nt], in1=m_all[:, 0:nt],
        op=ALU.mult))
    ph.on("vector", lambda e: e.tensor_scalar(
        out=fb["vt"][:, 0:nt], in0=sqs_all[:, 0:nt], scalar1=1.0 / 1024.0,
        scalar2=EPS, op0=ALU.mult, op1=ALU.add))
    ph.on("vector", lambda e: e.tensor_tensor(
        out=veps_all[:, 0:nt], in0=fb["vt"][:, 0:nt], in1=fb["p2"][:, 0:nt],
        op=ALU.subtract), inc=V)
    return V.v


def emit_rstd(ph, sm, veps_all, rstd_all, scr, lo, n, wait_v):
    V, Ssem = sm["sv"], sm["ss"]
    ph.on("scalar", lambda e: e.activation(
        out=scr[:, lo:lo + n], in_=veps_all[:, lo:lo + n], func=ACT.Sqrt),
        waits=[(V, wait_v)], inc=Ssem)
    sq = Ssem.v
    ph.on("vector", lambda e: e.reciprocal(
        out=rstd_all[:, lo:lo + n], in_=scr[:, lo:lo + n]),
        waits=[(Ssem, sq)], inc=V)
    return V.v


def build_ffn_phase(nc, sm, src_dram, dst_dram, wgT, wuT, wdT, m_all, rstd_all,
                    identb, tag, out_stats=None, pre_waits=(), dbg=None):
    sldp, sstp, sv, ss, st_ = sm["sldp"], sm["sstp"], sm["sv"], sm["ss"], sm["st"]
    with ExitStack() as es:
        ph = Ph()
        for eng in ENGS:
            ph.wait(eng, pre_waits)
        sb = lambda n, shp, dt: es.enter_context(nc.sbuf_tensor(tag + n, shp, dt))
        xin = [sb("f_xin0", [128, D], F32), sb("f_xin1", [128, D], F32)]
        rout = [sb("f_rout0", [128, D], F32), sb("f_rout1", [128, D], F32)]
        x_ln = sb("f_xln", [128, D], F32)
        xq = [sb("f_xq0", [128, D], BF16)] * 2
        xqT = [sb("f_xqT0", [128, NDC, 128], BF16)] * 2
        qb = mk_qdq_bufs(nc, es, tag + "q_", IHALF)
        silu = [sb("f_silu0", [128, 512], F32), sb("f_silu1", [128, 512], F32)]
        hhalf = sb("f_hh", [128, IHALF], F32)
        hq = sb("f_hq", [128, IHALF], BF16)
        hqT = sb("f_hqT", [128, NH, 128], BF16)


        p_tr = es.enter_context(nc.psum_tensor(tag + "ptr", [128, 8, 128], BF16))
        p_tr2 = es.enter_context(nc.psum_tensor(tag + "ptr2", [128, 8, 128], BF16))
        p_g = [es.enter_context(nc.psum_tensor(f"{tag}pg{i}", [128, 512], F32))
               for i in range(2)]
        p_u = [es.enter_context(nc.psum_tensor(f"{tag}pu{i}", [128, 512], F32))
               for i in range(2)]
        p_d = [es.enter_context(nc.psum_tensor(f"{tag}pd{i}", [128, 512], F32))
               for i in range(2)]

        emit_qdq_init(ph, qb)
        slabs = [(0, 512), (512, 512), (1024, 384)]
        last_xin_free = [0, 0]
        last_store = [0, 0]
        last_xq_use = [0, 0]
        last_xqT_use = [0, 0]
        last_tr_evac = 0
        last_tr2_evac = 0
        last_gu_free = [[0, 0], [0, 0]]
        last_hqT_use = 0
        last_pd_free = 0
        gslab = 0

        for t in range(NT):
            p = t % 2
            ph.on("sync", lambda e, t=t, p=p: e.dma_start(
                out=xin[p][:], in_=src_dram[t * 128:(t + 1) * 128, :]),
                waits=[(sv, last_xin_free[p])], inc=sldp[p], inc_by=16)
            ld_here = sldp[p].v
            ph.on("vector", lambda e, t=t, p=p: e.tensor_scalar(
                out=x_ln[:], in0=xin[p][:], scalar1=m_all[:, t:t + 1],
                scalar2=rstd_all[:, t:t + 1], op0=ALU.subtract, op1=ALU.mult),
                waits=[(sldp[p], ld_here)], inc=sv)
            ph.wait("vector", [(st_, last_xq_use[p])])
            if dbg is not None and t == 0:
                xln_v = sv.v
                ph.on("gpsimd", lambda e: e.dma_start(
                    out=dbg["xln"][:], in_=x_ln[:]), waits=[(sv, xln_v)],
                    inc=sstp[0], inc_by=16)
                ph.wait("vector", [(sstp[0], sstp[0].v)])
            xq_ready = emit_qdq(ph, sm, qb, x_ln[:], x_ln[:], xq[p][:], D)
            if dbg is not None and t == 0:
                ph.on("gpsimd", lambda e, p=p: e.dma_start(
                    out=dbg["xq"][:], in_=xq[p][:]), waits=[(sv, xq_ready)],
                    inc=sstp[0], inc_by=16)
                ph.wait("vector", [(sstp[0], sstp[0].v)])
            for k in range(NDC):
                ph.on("tensor", lambda e, k=k, p=p: e.transpose(
                    out=p_tr[:, k, :], in_=xq[p][:, k * 128:(k + 1) * 128],
                    identity=identb[:]),
                    waits=([(sv, xq_ready), (ss, last_tr_evac)] if k == 0 else ()),
                    inc=(st_ if k == NDC - 1 else None))
            last_xq_use[0] = last_xq_use[1] = st_.v
            trx_done = st_.v
            ph.on("scalar", lambda e, p=p: e.copy(out=xqT[p][:], in_=p_tr[:]),
                  waits=[(st_, trx_done), (st_, last_xqT_use[p])], inc=ss)
            last_tr_evac = ss.v
            xqT_ready = ss.v

            for half in range(2):
                ioff = half * IHALF
                for (soff, n) in slabs:
                    bp = gslab % 2
                    for k in range(NDC):
                        ph.on("tensor", lambda e, k=k, p=p, soff=soff, n=n, bp=bp, ioff=ioff: e.matmul(
                            out=p_g[bp][:, 0:n], lhsT=xqT[p][:, k, :],
                            rhs=wgT[:, k, ioff + soff:ioff + soff + n],
                            start=(k == 0), stop=(k == NDC - 1)),
                            waits=([(ss, xqT_ready), (sv, last_gu_free[0][bp])]
                                   if k == 0 else ()),
                            inc=(st_ if k == NDC - 1 else None))
                    g_done = st_.v
                    for k in range(NDC):
                        ph.on("tensor", lambda e, k=k, p=p, soff=soff, n=n, bp=bp, ioff=ioff: e.matmul(
                            out=p_u[bp][:, 0:n], lhsT=xqT[p][:, k, :],
                            rhs=wuT[:, k, ioff + soff:ioff + soff + n],
                            start=(k == 0), stop=(k == NDC - 1)),
                            waits=([(sv, last_gu_free[1][bp])] if k == 0 else ()),
                            inc=(st_ if k == NDC - 1 else None))
                    u_done = st_.v
                    ph.on("scalar", lambda e, bp=bp, n=n: e.activation(
                        out=silu[bp][:, 0:n], in_=p_g[bp][:, 0:n], func=ACT.Silu),
                        waits=[(st_, g_done)], inc=ss)
                    silu_done = ss.v
                    ph.on("vector", lambda e, bp=bp, n=n, soff=soff: e.tensor_tensor(
                        out=hhalf[:, soff:soff + n], in0=silu[bp][:, 0:n],
                        in1=p_u[bp][:, 0:n], op=ALU.mult),
                        waits=[(ss, silu_done), (st_, u_done)], inc=sv)
                    last_gu_free[0][bp] = sv.v
                    last_gu_free[1][bp] = sv.v
                    gslab += 1
                    if half == 1:
                        last_xqT_use[0] = last_xqT_use[1] = u_done
                ph.wait("vector", [(st_, last_hqT_use)])
                if dbg is not None and t == 0 and half == 0:
                    hh_v = sv.v
                    ph.on("gpsimd", lambda e: e.dma_start(
                        out=dbg["hh"][:], in_=hhalf[:]), waits=[(sv, hh_v)],
                        inc=sstp[0], inc_by=16)
                    ph.wait("vector", [(sstp[0], sstp[0].v)])
                hq_ready = emit_qdq(ph, sm, qb, hhalf[:], hhalf[:], hq[:], IHALF)
                if dbg is not None and t == 0 and half == 0:
                    ph.on("gpsimd", lambda e: e.dma_start(
                        out=dbg["hq"][:], in_=hq[:]), waits=[(sv, hq_ready)],
                        inc=sstp[0], inc_by=16)
                    ph.wait("vector", [(sstp[0], sstp[0].v)])
                for j in range(NH):
                    tgt = p_tr[:, j, :] if j < 8 else p_tr2[:, j - 8, :]
                    w0 = [(sv, hq_ready), (ss, last_tr_evac)]
                    w8 = [(ss, last_tr2_evac)]
                    ph.on("tensor", lambda e, j=j, tgt=tgt: e.transpose(
                        out=tgt, in_=hq[:, j * 128:(j + 1) * 128],
                        identity=identb[:]),
                        waits=(w0 if j == 0 else (w8 if j == 8 else ())),
                        inc=(st_ if j == NH - 1 else None))
                trh_done = st_.v
                ph.on("scalar", lambda e: e.copy(
                    out=hqT[:, 0:8, :], in_=p_tr[:]),
                    waits=[(st_, trh_done), (st_, last_hqT_use)], inc=ss)
                last_tr_evac = ss.v
                ph.on("scalar", lambda e: e.copy(
                    out=hqT[:, 8:NH, :], in_=p_tr2[:, 0:3, :]), inc=ss)
                last_tr2_evac = ss.v
                hqT_ready = ss.v
                for dslab in range(2):
                    for j in range(NH):
                        w = []
                        if j == 0:
                            w = [(ss, hqT_ready)]
                            if half == 0 and dslab == 0:
                                w.append((sv, last_pd_free))
                        ph.on("tensor", lambda e, j=j, dslab=dslab, half=half: e.matmul(
                            out=p_d[dslab][:], lhsT=hqT[:, j, :],
                            rhs=wdT[:, half * NH + j,
                                    dslab * 512:(dslab + 1) * 512],
                            start=(half == 0 and j == 0),
                            stop=(half == 1 and j == NH - 1)),
                            waits=w,
                            inc=(st_ if (half == 1 and j == NH - 1) else None))
                last_hqT_use = st_.v
            down_done = st_.v
            ph.wait("vector", [(sstp[p], last_store[p])])
            for dslab in range(2):
                ph.on("vector", lambda e, dslab=dslab, p=p: e.tensor_tensor(
                    out=rout[p][:, dslab * 512:(dslab + 1) * 512],
                    in0=p_d[dslab][:],
                    in1=xin[p][:, dslab * 512:(dslab + 1) * 512], op=ALU.add),
                    waits=([(st_, down_done)] if dslab == 0 else ()), inc=sv)
            last_xin_free[p] = sv.v
            last_pd_free = sv.v
            resid_done = sv.v
            if out_stats is not None:
                o_sums, o_sqs = out_stats
                resid_done = emit_tile_stats(ph, sm, qb["fv2"], o_sums, o_sqs,
                                             t, rout[p][:],
                                             wait_src=[(sv, resid_done)])
            ph.on("gpsimd", lambda e, t=t, p=p: e.dma_start(
                out=dst_dram[t * 128:(t + 1) * 128, :], in_=rout[p][:]),
                waits=[(sv, resid_done)], inc=sstp[p], inc_by=16)
            last_store[p] = sstp[p].v
        ph.wait("gpsimd", [(sstp[0], sstp[0].v), (sstp[1], sstp[1].v)])
        ph.run(nc)


def build_program(debug=False):
    nc = bass.Bass("TRN2", target_bir_lowering=False)

    x_in = nc.dram_tensor("x_in", [TOK, D], F32, kind="ExternalInput").ap()
    wz2 = nc.dram_tensor("wz2", [D, D], F32, kind="ExternalInput").ap()
    wh2 = nc.dram_tensor("wh2", [D, D], F32, kind="ExternalInput").ap()
    wgd = nc.dram_tensor("wgd", [D, D], F32, kind="ExternalInput").ap()
    bzf = nc.dram_tensor("bzf", [D], F32, kind="ExternalInput").ap()
    bhf = nc.dram_tensor("bhf", [D], F32, kind="ExternalInput").ap()
    bgf = nc.dram_tensor("bgf", [D], F32, kind="ExternalInput").ap()
    wgate = nc.dram_tensor("wgate", [II, D], F32, kind="ExternalInput").ap()
    wup = nc.dram_tensor("wup", [II, D], F32, kind="ExternalInput").ap()
    wdown = nc.dram_tensor("wdown", [D, II], F32, kind="ExternalInput").ap()
    identb_d = nc.dram_tensor("identb_d", [128, 128], BF16, kind="ExternalInput").ap()
    identf_d = nc.dram_tensor("identf_d", [128, 128], F32, kind="ExternalInput").ap()
    out_d = nc.dram_tensor("out", [TOK, D], F32, kind="ExternalOutput").ap()
    kres = "ExternalOutput" if debug else "Internal"
    resid1 = nc.dram_tensor("resid1", [TOK, D], F32, kind=kres).ap()
    resid2 = nc.dram_tensor("resid2", [TOK, D], F32, kind=kres).ap()
    wg_st = nc.dram_tensor("wg_st", [NDC * 128, II], BF16, kind="Internal").ap()
    dbg = None
    if debug:
        dbg = {
            "xln": nc.dram_tensor("dbg_xln", [128, D], F32, kind="ExternalOutput").ap(),
            "xq": nc.dram_tensor("dbg_xq", [128, D], BF16, kind="ExternalOutput").ap(),
            "hh": nc.dram_tensor("dbg_hh", [128, IHALF], F32, kind="ExternalOutput").ap(),
            "hq": nc.dram_tensor("dbg_hq", [128, IHALF], BF16, kind="ExternalOutput").ap(),
            "wgT": nc.dram_tensor("dbg_wgT", [128, NDC, II], BF16, kind="ExternalOutput").ap(),
            "wdT": nc.dram_tensor("dbg_wdT", [128, NIC, D], BF16, kind="ExternalOutput").ap(),
            "mx": nc.dram_tensor("dbg_mx", [128, NT], F32, kind="ExternalOutput").ap(),
            "rstdx": nc.dram_tensor("dbg_rstdx", [128, NT], F32, kind="ExternalOutput").ap(),
            "wzb": nc.dram_tensor("dbg_wzb", [128, NDC, D], BF16, kind="ExternalOutput").ap(),
            "xlnT": nc.dram_tensor("dbg_xlnT", [128, NDC, S], BF16, kind="ExternalOutput").ap(),
            "abuf": nc.dram_tensor("dbg_abuf", [128, S], F32, kind="ExternalOutput").ap(),
            "bbuf": nc.dram_tensor("dbg_bbuf", [128, S], F32, kind="ExternalOutput").ap(),
            "h0": nc.dram_tensor("dbg_h0", [128, S], F32, kind="ExternalOutput").ap(),
            "hall": nc.dram_tensor("dbg_hall", [128, NDC, S], F32, kind="ExternalOutput").ap(),
            "rst1": nc.dram_tensor("dbg_rst1", [128, NT], F32, kind="ExternalOutput").ap(),
            "hb": nc.dram_tensor("dbg_hb", [128, NDC, 512], BF16, kind="ExternalOutput").ap(),
            "g": nc.dram_tensor("dbg_g", [128, NDC, 512], F32, kind="ExternalOutput").ap(),
            "ao": nc.dram_tensor("dbg_ao", [128, NDC, 512], F32, kind="ExternalOutput").ap(),
            "taoT": nc.dram_tensor("dbg_taoT", [128, 2, 4, 128], F32, kind="ExternalOutput").ap(),
            "wzb2": nc.dram_tensor("dbg_wzb2", [128, NDC, D], BF16, kind="ExternalOutput").ap(),
            "wgb2": nc.dram_tensor("dbg_wgb2", [128, NDC, D], BF16, kind="ExternalOutput").ap(),
            "pgt0": nc.dram_tensor("dbg_pgt0", [128, 512], F32, kind="ExternalOutput").ap(),
            "pgt1": nc.dram_tensor("dbg_pgt1", [128, 512], F32, kind="ExternalOutput").ap(),
            "wgb1": nc.dram_tensor("dbg_wgb1", [128, NDC, D], BF16, kind="ExternalOutput").ap(),
            "wgb15": nc.dram_tensor("dbg_wgb15", [128, NDC, D], BF16, kind="ExternalOutput").ap(),
            "bzs": nc.dram_tensor("dbg_bzs", [128, NDC], F32, kind="ExternalOutput").ap(),
            "st1": nc.dram_tensor("dbg_st1", [128, NT * 12], F32, kind="ExternalOutput").ap(),
            "m1": nc.dram_tensor("dbg_m1", [128, NT], F32, kind="ExternalOutput").ap(),
            "v1": nc.dram_tensor("dbg_v1", [128, NT], F32, kind="ExternalOutput").ap(),
            "fsq": nc.dram_tensor("dbg_fsq", [128, NT, 4], F32, kind="ExternalOutput").ap(),
            "fms": nc.dram_tensor("dbg_fms", [128, NT], F32, kind="ExternalOutput").ap(),
        }
    wu_st = nc.dram_tensor("wu_st", [NDC * 128, II], BF16, kind="Internal").ap()
    wd_st = nc.dram_tensor("wd_st", [NIC * 128, D], BF16, kind="Internal").ap()

    prog = ExitStack()
    with prog:
        sm = {}
        for n in ("sld0", "sld1", "sst0", "sst1", "smisc", "sv", "ss", "st"):
            sm[n] = Sem(prog.enter_context(nc.semaphore(n)), n)
        sldp = [sm["sld0"], sm["sld1"]]
        sstp = [sm["sst0"], sm["sst1"]]
        smisc = sm["smisc"]
        sm["sldp"] = sldp
        sm["sstp"] = sstp
        sv, ss, st_ = sm["sv"], sm["ss"], sm["st"]

        identb = prog.enter_context(nc.sbuf_tensor("identb_s", [128, 128], BF16))
        identf = prog.enter_context(nc.sbuf_tensor("identf_s", [128, 128], F32))
        m1_all = prog.enter_context(nc.sbuf_tensor("m1_all", [128, NT], F32))
        veps1_all = prog.enter_context(nc.sbuf_tensor("veps1_all", [128, NT], F32))
        rstd1_all = prog.enter_context(nc.sbuf_tensor("rstd1_all", [128, NT], F32))
        m2_all = prog.enter_context(nc.sbuf_tensor("m2_all", [128, NT], F32))
        veps2_all = prog.enter_context(nc.sbuf_tensor("veps2_all", [128, NT], F32))
        rstd2_all = prog.enter_context(nc.sbuf_tensor("rstd2_all", [128, NT], F32))
        mx_all = prog.enter_context(nc.sbuf_tensor("mx_all", [128, NT], F32))
        vepsx_all = prog.enter_context(nc.sbuf_tensor("vepsx_all", [128, NT], F32))
        rstdx_all = prog.enter_context(nc.sbuf_tensor("rstdx_all", [128, NT], F32))
        scr_all = prog.enter_context(nc.sbuf_tensor("scr_all", [128, NT], F32))
        sums1 = prog.enter_context(nc.sbuf_tensor("sums1", [128, NT], F32))
        sqs1 = prog.enter_context(nc.sbuf_tensor("sqs1", [128, NT], F32))
        sums2 = prog.enter_context(nc.sbuf_tensor("sums2", [128, NT], F32))
        sqs2 = prog.enter_context(nc.sbuf_tensor("sqs2", [128, NT], F32))

        with ExitStack() as es:
            ph = Ph()
            ph.on("sync", lambda e: e.dma_start(out=identb[:], in_=identb_d[:]),
                  inc=smisc, inc_by=16)
            ph.on("sync", lambda e: e.dma_start(out=identf[:], in_=identf_d[:]),
                  inc=smisc, inc_by=16)
            xa = [es.enter_context(nc.sbuf_tensor(f"pa_x{i}", [128, D], F32))
                  for i in range(2)]
            sumsx = es.enter_context(nc.sbuf_tensor("sumsx", [128, NT], F32))
            sqsx = es.enter_context(nc.sbuf_tensor("sqsx", [128, NT], F32))
            x2scr = es.enter_context(nc.sbuf_tensor("pa_x2", [128, D], F32))
            fb = mk_fin_bufs(nc, es, "pa_", NT)
            done_v = [0, 0]
            for t in range(NT):
                p = t % 2
                ph.on("sync", lambda e, t=t, p=p: e.dma_start(
                    out=xa[p][:], in_=x_in[t * 128:(t + 1) * 128, :]),
                    waits=[(sv, done_v[p])], inc=sldp[p], inc_by=16)
                done_v[p] = emit_tile_stats(ph, sm, x2scr, sumsx, sqsx, t,
                                            xa[p][:],
                                            wait_src=[(sldp[p], sldp[p].v)])
            fv_ = emit_stats_final(ph, sm, fb, sumsx, sqsx, mx_all, vepsx_all,
                                   NT, sv.v)
            rv = emit_rstd(ph, sm, vepsx_all, rstdx_all, scr_all, 0, NT, fv_)
            if dbg is not None:
                ph.on("gpsimd", lambda e: e.dma_start(
                    out=dbg["mx"][:], in_=mx_all[:]), waits=[(sv, rv)],
                    inc=sstp[0], inc_by=16)
                ph.on("gpsimd", lambda e: e.dma_start(
                    out=dbg["rstdx"][:], in_=rstdx_all[:]), inc=sstp[0], inc_by=16)
            ph.wait("gpsimd", [(sstp[0], sstp[0].v)])
            ph.run(nc)

        with ExitStack() as esw:
            wgT = esw.enter_context(nc.sbuf_tensor("wgT", [128, NDC, II], BF16))
            wuT = esw.enter_context(nc.sbuf_tensor("wuT", [128, NDC, II], BF16))
            wdT = esw.enter_context(nc.sbuf_tensor("wdT", [128, NIC, D], BF16))

            with ExitStack() as es:
                ph = Ph()
                ph.wait("tensor", [(smisc, 32)])
                wst = [es.enter_context(nc.sbuf_tensor(f"wp_st{i}", [128, II], F32))
                       for i in range(2)]
                qb = mk_qdq_bufs(nc, es, "wp_", IHALF)
                wq = es.enter_context(nc.sbuf_tensor("wp_wq", [128, II], BF16))
                p_tr = es.enter_context(nc.psum_tensor("wp_ptr", [128, 8, 128], BF16))
                p_tr2 = es.enter_context(nc.psum_tensor("wp_ptr2", [128, 8, 128], BF16))
                emit_qdq_init(ph, qb)
                state = {"tr": 0, "tr2": 0, "wv": [0, 0], "trT": 0}

                def prep_gu(wdram, wT):
                    for c in range(NIC):
                        p = c % 2
                        ph.on("sync", lambda e, c=c, p=p, wdram=wdram: e.dma_start(
                            out=wst[p][:, 0:D], in_=wdram[c * 128:(c + 1) * 128, :]),
                            waits=[(sv, state["wv"][p])], inc=sldp[p], inc_by=16)
                        ph.wait("vector", [(sldp[p], sldp[p].v), (st_, state["trT"])])
                        q_ready = emit_qdq(ph, sm, qb, wst[p][:, 0:D],
                                           wst[p][:, 0:D], wq[:, 0:D], D)
                        state["wv"][p] = q_ready
                        for k in range(NDC):
                            ph.on("tensor", lambda e, k=k: e.transpose(
                                out=p_tr[:, k, :],
                                in_=wq[:, k * 128:(k + 1) * 128],
                                identity=identb[:]),
                                waits=([(sv, q_ready), (ss, state["tr"])]
                                       if k == 0 else ()),
                                inc=(st_ if k == NDC - 1 else None))
                        tr_done = st_.v
                        state["trT"] = st_.v
                        ph.on("scalar", lambda e, c=c, wT=wT: e.copy(
                            out=wT[:, 0:NDC, c * 128:(c + 1) * 128], in_=p_tr[:]),
                            waits=[(st_, tr_done)], inc=ss)
                        state["tr"] = ss.v

                def prep_down():
                    for c in range(NDC):
                        p = c % 2
                        ph.on("sync", lambda e, c=c, p=p: e.dma_start(
                            out=wst[p][:], in_=wdown[c * 128:(c + 1) * 128, :]),
                            waits=[(sv, state["wv"][p])], inc=sldp[p], inc_by=16)
                        ph.wait("vector", [(sldp[p], sldp[p].v), (st_, state["trT"])])
                        emit_qdq(ph, sm, qb, wst[p][:, 0:IHALF],
                                 wst[p][:, 0:IHALF], wq[:, 0:IHALF], IHALF,
                                 fold=0.5)
                        q_ready = emit_qdq(ph, sm, qb, wst[p][:, IHALF:II],
                                           wst[p][:, IHALF:II], wq[:, IHALF:II],
                                           IHALF, fold=0.5)
                        state["wv"][p] = q_ready
                        for grp, (j0, nj) in enumerate(((0, 8), (8, 8), (16, 6))):
                            ptr = p_tr if grp != 1 else p_tr2
                            trs = state["tr"] if grp != 1 else state["tr2"]
                            for j in range(nj):
                                ph.on("tensor", lambda e, j=j, j0=j0, ptr=ptr: e.transpose(
                                    out=ptr[:, j, :],
                                    in_=wq[:, (j0 + j) * 128:(j0 + j + 1) * 128],
                                    identity=identb[:]),
                                    waits=([(sv, q_ready), (ss, trs)]
                                           if j == 0 else ()),
                                    inc=(st_ if j == nj - 1 else None))
                            tr_done = st_.v
                            state["trT"] = st_.v
                            ph.on("scalar", lambda e, c=c, j0=j0, nj=nj, ptr=ptr: e.copy(
                                out=wdT[:, j0:j0 + nj, c * 128:(c + 1) * 128],
                                in_=ptr[:, 0:nj, :]),
                                waits=[(st_, tr_done)], inc=ss)
                            if grp != 1:
                                state["tr"] = ss.v
                            else:
                                state["tr2"] = ss.v

                prep_gu(wgate, wgT)
                prep_gu(wup, wuT)
                prep_down()
                wv = ss.v
                ph.on("gpsimd", lambda e: e.dma_start(
                    out=wg_st.rearrange("(k p) i -> p k i", p=128), in_=wgT[:]),
                    waits=[(ss, wv)], inc=smisc, inc_by=16)
                ph.on("gpsimd", lambda e: e.dma_start(
                    out=wu_st.rearrange("(k p) i -> p k i", p=128), in_=wuT[:]),
                    inc=smisc, inc_by=16)
                ph.on("gpsimd", lambda e: e.dma_start(
                    out=wd_st.rearrange("(k p) i -> p k i", p=128), in_=wdT[:]),
                    inc=smisc, inc_by=16)
                if dbg is not None:
                    wv2 = ss.v
                    ph.on("gpsimd", lambda e: e.dma_start(
                        out=dbg["wgT"][:], in_=wgT[:]), waits=[(ss, wv2)],
                        inc=smisc, inc_by=16)
                    ph.on("gpsimd", lambda e: e.dma_start(
                        out=dbg["wdT"][:], in_=wdT[:]), inc=smisc, inc_by=16)
                sm["stash_done"] = smisc.v
                ph.wait("gpsimd", [(smisc, smisc.v)])
                ph.run(nc)

            build_ffn_phase(nc, sm, x_in, resid1, wgT[:], wuT[:], wdT[:],
                            mx_all, rstdx_all, identb, "f1_",
                            out_stats=(sums1, sqs1), dbg=dbg)

        with ExitStack() as esa:
            wzb = esa.enter_context(nc.sbuf_tensor("wzb", [128, NDC, D], BF16))
            whb = esa.enter_context(nc.sbuf_tensor("whb", [128, NDC, D], BF16))
            wgb = esa.enter_context(nc.sbuf_tensor("wgb", [128, NDC, D], BF16))
            bz_s = esa.enter_context(nc.sbuf_tensor("bz_s", [128, NDC], F32))
            nbz_s = esa.enter_context(nc.sbuf_tensor("nbz_s", [128, NDC], F32))
            bh_s = esa.enter_context(nc.sbuf_tensor("bh_s", [128, NDC], F32))
            bg_s = esa.enter_context(nc.sbuf_tensor("bg_s", [128, NDC], F32))

            with ExitStack() as es:
                ph = Ph()
                wstg = [es.enter_context(nc.sbuf_tensor(f"aw{i}", [128, D], F32))
                        for i in range(2)]
                cnt = {"v": [0, 0]}
                for wi, (wsrc, wdst) in enumerate(((wz2, wzb), (wh2, whb), (wgd, wgb))):
                    for c in range(NDC):
                        p = (wi * NDC + c) % 2
                        ph.on("sync", lambda e, c=c, p=p, wsrc=wsrc: e.dma_start(
                            out=wstg[p][:], in_=wsrc[c * 128:(c + 1) * 128, :]),
                            waits=[(sv, cnt["v"][p])], inc=sldp[p], inc_by=16)
                        ph.on("vector", lambda e, c=c, p=p, wdst=wdst: e.tensor_copy(
                            out=wdst[:, c, :], in_=wstg[p][:]),
                            waits=[(sldp[p], sldp[p].v)], inc=sv)
                        cnt["v"][p] = sv.v
                for bsrc, bdst in ((bzf, bz_s), (bhf, bh_s), (bgf, bg_s)):
                    ph.on("sync", lambda e, bsrc=bsrc, bdst=bdst: e.dma_start(
                        out=bdst[:], in_=bsrc.rearrange("(c p) -> p c", p=128),
                        allow_slow_non_contiguous=True),
                        inc=smisc, inc_by=16)
                ph.on("vector", lambda e: e.tensor_scalar(
                    out=nbz_s[:], in0=bz_s[:], scalar1=-1.0, scalar2=None,
                    op0=ALU.mult), waits=[(smisc, smisc.v)], inc=sv)
                if dbg is not None:
                    wv3 = sv.v
                    ph.on("gpsimd", lambda e: e.dma_start(
                        out=dbg["st1"][:, 0:NT], in_=sums1[:]), waits=[(sv, wv3)],
                        inc=sstp[0], inc_by=16)
                    ph.on("gpsimd", lambda e: e.dma_start(
                        out=dbg["wzb"][:], in_=wzb[:]),
                        inc=sstp[0], inc_by=16)
                    ph.on("gpsimd", lambda e: e.dma_start(
                        out=dbg["wgb1"][:], in_=wgb[:]), inc=sstp[0], inc_by=16)
                    ph.on("gpsimd", lambda e: e.dma_start(
                        out=dbg["bzs"][:], in_=bz_s[:]), inc=sstp[0], inc_by=16)
                fb1 = mk_fin_bufs(nc, es, "d_", NT)
                fv1 = emit_stats_final(ph, sm, fb1, sums1, sqs1, m1_all,
                                       veps1_all, NT, sv.v)
                if dbg is not None:
                    ph.on("gpsimd", lambda e: e.dma_start(
                        out=dbg["m1"][:], in_=m1_all[:]), waits=[(sv, fv1)],
                        inc=sstp[0], inc_by=16)
                    ph.on("gpsimd", lambda e: e.dma_start(
                        out=dbg["v1"][:], in_=veps1_all[:]), inc=sstp[0], inc_by=16)
                    ph.on("gpsimd", lambda e: e.dma_start(
                        out=dbg["fms"][:], in_=fb1["p2"][:]), inc=sstp[0], inc_by=16)
                rrv = emit_rstd(ph, sm, veps1_all, rstd1_all, scr_all, 0, NT, fv1)
                if dbg is not None:
                    ph.on("gpsimd", lambda e: e.dma_start(
                        out=dbg["rst1"][:], in_=rstd1_all[:]), waits=[(sv, rrv)],
                        inc=sstp[0], inc_by=16)
                ph.wait("gpsimd", [(sstp[0], sstp[0].v)])
                ph.run(nc)

            for s in range(SEQ_PC):
                with ExitStack() as esq:
                    h_t = esq.enter_context(
                        nc.sbuf_tensor(f"a_h{s}", [128, NDC, S], F32))
                    with ExitStack() as es:
                        ph = Ph()
                        xlnT = es.enter_context(
                            nc.sbuf_tensor(f"a_xlnT{s}", [128, NDC, S], BF16))
                        xl_bf = es.enter_context(
                            nc.sbuf_tensor(f"a_xlbf{s}", [128, D], BF16))
                        rtile = [es.enter_context(
                            nc.sbuf_tensor(f"a_rt{s}_{i}", [128, D], F32))
                            for i in range(2)]
                        a_buf = es.enter_context(
                            nc.sbuf_tensor(f"a_a{s}", [128, S], F32))
                        bb_buf = es.enter_context(
                            nc.sbuf_tensor(f"a_bb{s}", [128, S], F32))
                        z_sb = [es.enter_context(
                            nc.sbuf_tensor(f"a_z{s}_{i}", [128, 512], F32))
                            for i in range(2)]
                        p_tra = es.enter_context(
                            nc.psum_tensor(f"a_ptra{s}", [128, 8, 128], BF16))
                        p_z = [es.enter_context(
                            nc.psum_tensor(f"a_pz{s}_{i}", [128, 512], F32))
                            for i in range(2)]
                        p_ht = [es.enter_context(
                            nc.psum_tensor(f"a_pht{s}_{i}", [128, 512], F32))
                            for i in range(2)]
                        lva = [0, 0]
                        evA = 0
                        vbf = 0
                        for q in range(TPS):
                            g = s * TPS + q
                            p = q % 2
                            ph.on("sync", lambda e, g=g, p=p: e.dma_start(
                                out=rtile[p][:],
                                in_=resid1[g * 128:(g + 1) * 128, :]),
                                waits=[(sv, lva[p])],
                                inc=sldp[p], inc_by=16)
                            ph.on("vector", lambda e, g=g, p=p: e.tensor_scalar(
                                out=xl_bf[:], in0=rtile[p][:],
                                scalar1=m1_all[:, g:g + 1],
                                scalar2=rstd1_all[:, g:g + 1],
                                op0=ALU.subtract, op1=ALU.mult),
                                waits=[(sldp[p], sldp[p].v), (st_, vbf)], inc=sv)
                            lva[p] = sv.v
                            xl_ready = sv.v
                            for k in range(NDC):
                                ph.on("tensor", lambda e, k=k: e.transpose(
                                    out=p_tra[:, k, :],
                                    in_=xl_bf[:, k * 128:(k + 1) * 128],
                                    identity=identb[:]),
                                    waits=([(sv, xl_ready), (ss, evA)]
                                           if k == 0 else ()),
                                    inc=(st_ if k == NDC - 1 else None))
                            vbf = st_.v
                            ph.on("scalar", lambda e, q=q: e.copy(
                                out=xlnT[:, 0:NDC, q * 128:(q + 1) * 128],
                                in_=p_tra[:]), waits=[(st_, st_.v)], inc=ss)
                            evA = ss.v
                        xlnT_ready = ss.v
                        if dbg is not None and s == DBG_SEQ:
                            ph.on("gpsimd", lambda e: e.dma_start(
                                out=dbg["xlnT"][:], in_=xlnT[:]),
                                waits=[(ss, xlnT_ready)], inc=sstp[0], inc_by=16)
                        zfree = [0, 0]
                        zuse = [0, 0]
                        htfree = [0, 0]
                        last_scan = 0
                        for c in range(NDC):
                            sa_done = 0
                            for sl in range(4):
                                bp = sl % 2
                                for k in range(NDC):
                                    ph.on("tensor", lambda e, k=k, c=c, sl=sl, bp=bp: e.matmul(
                                        out=p_z[bp][:], lhsT=wzb[:, k, c * 128:(c + 1) * 128],
                                        rhs=xlnT[:, k, sl * 512:(sl + 1) * 512],
                                        start=(k == 0), stop=(k == NDC - 1)),
                                        waits=([(ss, xlnT_ready), (ss, zfree[bp])]
                                               if k == 0 else ()),
                                        inc=(st_ if k == NDC - 1 else None))
                                z_done = st_.v
                                for k in range(NDC):
                                    ph.on("tensor", lambda e, k=k, c=c, sl=sl, bp=bp: e.matmul(
                                        out=p_ht[bp][:], lhsT=whb[:, k, c * 128:(c + 1) * 128],
                                        rhs=xlnT[:, k, sl * 512:(sl + 1) * 512],
                                        start=(k == 0), stop=(k == NDC - 1)),
                                        waits=([(sv, htfree[bp])] if k == 0 else ()),
                                        inc=(st_ if k == NDC - 1 else None))
                                ht_done = st_.v
                                ph.on("scalar", lambda e, bp=bp, c=c: e.activation(
                                    out=z_sb[bp][:], in_=p_z[bp][:],
                                    func=ACT.Sigmoid, bias=bz_s[:, c:c + 1]),
                                    waits=[(st_, z_done), (sv, zuse[bp])], inc=ss)
                                zs_done = ss.v
                                ph.on("scalar", lambda e, bp=bp, c=c, sl=sl: e.activation(
                                    out=a_buf[:, sl * 512:(sl + 1) * 512],
                                    in_=p_z[bp][:], func=ACT.Sigmoid,
                                    bias=nbz_s[:, c:c + 1], scale=-1.0),
                                    waits=[(sv, last_scan)], inc=ss)
                                zfree[bp] = ss.v
                                sa_done = ss.v
                                ph.on("vector", lambda e, bp=bp, c=c, sl=sl: e.scalar_tensor_tensor(
                                    out=bb_buf[:, sl * 512:(sl + 1) * 512],
                                    in0=p_ht[bp][:], scalar=bh_s[:, c:c + 1],
                                    in1=z_sb[bp][:], op0=ALU.add, op1=ALU.mult),
                                    waits=[(ss, zs_done), (st_, ht_done)], inc=sv)
                                htfree[bp] = sv.v
                                zuse[bp] = sv.v
                            if dbg is not None and s == DBG_SEQ and c == 0:
                                av = sa_done
                                bv = sv.v
                                ph.on("gpsimd", lambda e: e.dma_start(
                                    out=dbg["abuf"][:], in_=a_buf[:]),
                                    waits=[(ss, av), (sv, bv)], inc=sstp[0], inc_by=16)
                                ph.on("gpsimd", lambda e: e.dma_start(
                                    out=dbg["bbuf"][:], in_=bb_buf[:]),
                                    inc=sstp[0], inc_by=16)
                                ph.wait("vector", [(sstp[0], sstp[0].v)])
                            ph.on("vector", lambda e, c=c: e.tensor_tensor_scan(
                                out=h_t[:, c, :], data0=a_buf[:], data1=bb_buf[:],
                                initial=0.0, op0=ALU.mult, op1=ALU.add),
                                waits=[(ss, sa_done)], inc=sv)
                            last_scan = sv.v
                            if dbg is not None and s == DBG_SEQ and c == NDC - 1:
                                ph.on("gpsimd", lambda e: e.dma_start(
                                    out=dbg["wgb15"][:], in_=wgb[:]),
                                    waits=[(sv, sv.v)], inc=sstp[0], inc_by=16)
                            if dbg is not None and s == DBG_SEQ and c == 0:
                                ph.on("gpsimd", lambda e: e.dma_start(
                                    out=dbg["h0"][:], in_=h_t[:, 0, :]),
                                    waits=[(sv, sv.v)], inc=sstp[0], inc_by=16)
                        if dbg is not None and s == DBG_SEQ:
                            ph.on("gpsimd", lambda e: e.dma_start(
                                out=dbg["hall"][:], in_=h_t[:]),
                                waits=[(sv, sv.v)], inc=sstp[0], inc_by=16)
                        ph.wait("gpsimd", [(sstp[0], sstp[0].v)])
                        ph.run(nc)

                    with ExitStack() as es:
                        ph = Ph()
                        hb_sl = es.enter_context(
                            nc.sbuf_tensor(f"c_hb{s}", [128, NDC, 512], BF16))
                        g_sb = [es.enter_context(
                            nc.sbuf_tensor(f"c_g{s}_{i}", [128, 512], F32))
                            for i in range(2)]
                        ao = es.enter_context(
                            nc.sbuf_tensor(f"c_ao{s}", [128, NDC, 512], F32))
                        rtile = [es.enter_context(
                            nc.sbuf_tensor(f"c_rt{s}_{i}", [128, D], F32))
                            for i in range(2)]
                        rout2 = [es.enter_context(
                            nc.sbuf_tensor(f"c_ro{s}_{i}", [128, D], F32))
                            for i in range(2)]
                        x2c = es.enter_context(
                            nc.sbuf_tensor(f"c_x2_{s}", [128, D], F32))
                        p_gt = [es.enter_context(
                            nc.psum_tensor(f"c_pgt{s}_{i}", [128, 512], F32))
                            for i in range(2)]
                        p_tao = [es.enter_context(
                            nc.psum_tensor(f"c_ptao{s}_{i}", [128, 4, 128], F32))
                            for i in range(2)]
                        scan_all = sv.v
                        gfree = [0, 0]
                        hb_free = 0
                        ao_free = 0
                        tao_free = [0, 0]
                        lvc = [0, 0]
                        stc = [0, 0]
                        for sl in range(4):
                            for k in range(NDC):
                                ph.on("scalar", lambda e, k=k, sl=sl: e.copy(
                                    out=hb_sl[:, k, :],
                                    in_=h_t[:, k, sl * 512:(sl + 1) * 512]),
                                    waits=([(sv, scan_all), (st_, hb_free)]
                                           if k == 0 else ()), inc=ss)
                            hb_ready = ss.v
                            if dbg is not None and s == DBG_SEQ and sl == 0:
                                ph.on("gpsimd", lambda e: e.dma_start(
                                    out=dbg["hb"][:], in_=hb_sl[:]),
                                    waits=[(ss, hb_ready)], inc=sstp[0], inc_by=16)
                            for m in range(NDC):
                                bp = m % 2
                                for k in range(NDC):
                                    ph.on("tensor", lambda e, k=k, m=m, bp=bp: e.matmul(
                                        out=p_gt[bp][:], lhsT=wgb[:, k, m * 128:(m + 1) * 128],
                                        rhs=hb_sl[:, k, :],
                                        start=(k == 0), stop=(k == NDC - 1)),
                                        waits=([(ss, hb_ready), (sv, gfree[bp])]
                                               if k == 0 else ()),
                                        inc=(st_ if k == NDC - 1 else None))
                                gt_done = st_.v
                                if m == NDC - 1:
                                    hb_free = st_.v
                                ph.on("scalar", lambda e, bp=bp, m=m: e.activation(
                                    out=g_sb[bp][:], in_=p_gt[bp][:],
                                    func=ACT.Sigmoid, bias=bg_s[:, m:m + 1]),
                                    waits=[(st_, gt_done)], inc=ss)
                                gs_done = ss.v
                                ph.on("vector", lambda e, bp=bp, m=m, sl=sl: e.tensor_tensor(
                                    out=ao[:, m, :], in0=g_sb[bp][:],
                                    in1=h_t[:, m, sl * 512:(sl + 1) * 512],
                                    op=ALU.mult),
                                    waits=[(ss, gs_done), (st_, ao_free)], inc=sv)
                                gfree[bp] = sv.v
                                if dbg is not None and s == DBG_SEQ and sl == 0:
                                    ph.on("gpsimd", lambda e, m=m: e.dma_start(
                                        out=dbg["g"][:, m, :], in_=g_sb[bp][:]),
                                        waits=[(ss, gs_done)], inc=sstp[0], inc_by=16)
                                    ph.wait("scalar", [(sstp[0], sstp[0].v)])
                            ao_ready = sv.v
                            if dbg is not None and s == DBG_SEQ and sl == 0:
                                ph.on("gpsimd", lambda e: e.dma_start(
                                    out=dbg["ao"][:], in_=ao[:]),
                                    waits=[(sv, ao_ready)], inc=sstp[0], inc_by=16)
                            for q in range(4):
                                g = s * TPS + sl * 4 + q
                                p = q % 2
                                ph.on("sync", lambda e, g=g, p=p: e.dma_start(
                                    out=rtile[p][:],
                                    in_=resid1[g * 128:(g + 1) * 128, :]),
                                    waits=[(sv, lvc[p])], inc=sldp[p], inc_by=16)
                                ld_here = sldp[p].v
                                for m in range(NDC):
                                    ph.on("tensor", lambda e, m=m, q=q: e.matmul(
                                        out=p_tao[m // 4][:, m % 4, :],
                                        lhsT=ao[:, m, q * 128:(q + 1) * 128],
                                        rhs=identf[:], start=True, stop=True),
                                        waits=([(sv, ao_ready),
                                                (sv, tao_free[0]),
                                                (sv, tao_free[1])]
                                               if m == 0 else ()),
                                        inc=(st_ if m == NDC - 1 else None))
                                if q == 3:
                                    ao_free = st_.v
                                tao_done = st_.v
                                if dbg is not None and s == DBG_SEQ and sl == 0 and q == 0:
                                    ph.on("vector", lambda e: e.tensor_copy(
                                        out=rout2[1][:, 0:512],
                                        in_=p_tao[0][:].rearrange("p a b -> p (a b)")),
                                        waits=[(st_, tao_done)], inc=sv)
                                    ph.on("vector", lambda e: e.tensor_copy(
                                        out=rout2[1][:, 512:1024],
                                        in_=p_tao[1][:].rearrange("p a b -> p (a b)")), inc=sv)
                                    ph.on("gpsimd", lambda e: e.dma_start(
                                        out=dbg["taoT"][:].rearrange("p a b c -> p (a b c)"),
                                        in_=rout2[1][:]),
                                        waits=[(sv, sv.v)], inc=sstp[0], inc_by=16)
                                    ph.wait("vector", [(sstp[0], sstp[0].v)])
                                ph.wait("vector", [(sstp[p], stc[p])])
                                for half2 in range(2):
                                    ph.on("vector", lambda e, half2=half2, p=p: e.tensor_tensor(
                                        out=rout2[p][:, half2 * 512:(half2 + 1) * 512],
                                        in0=p_tao[half2][:].rearrange("p a b -> p (a b)"),
                                        in1=rtile[p][:, half2 * 512:(half2 + 1) * 512],
                                        op=ALU.add),
                                        waits=([(st_, tao_done), (sldp[p], ld_here)]
                                               if half2 == 0 else ()), inc=sv)
                                lvc[p] = sv.v
                                tao_free[0] = sv.v
                                tao_free[1] = sv.v
                                rdone = emit_tile_stats(
                                    ph, sm, x2c, sums2, sqs2, g, rout2[p][:],
                                    wait_src=[(sv, sv.v)])
                                ph.on("gpsimd", lambda e, g=g, p=p: e.dma_start(
                                    out=resid2[g * 128:(g + 1) * 128, :],
                                    in_=rout2[p][:]),
                                    waits=[(sv, rdone)], inc=sstp[p], inc_by=16)
                                stc[p] = sstp[p].v
                        if dbg is not None and s == SEQ_PC - 1:
                            ph.on("gpsimd", lambda e: e.dma_start(
                                out=dbg["wzb2"][:], in_=wzb[:]), inc=sstp[0], inc_by=16)
                            ph.on("gpsimd", lambda e: e.dma_start(
                                out=dbg["wgb2"][:], in_=wgb[:]), inc=sstp[0], inc_by=16)
                        ph.wait("gpsimd", [(sstp[0], sstp[0].v), (sstp[1], sstp[1].v)])
                        ph.run(nc)

        with ExitStack() as esw:
            wgT = esw.enter_context(nc.sbuf_tensor("wgT2", [128, NDC, II], BF16))
            wuT = esw.enter_context(nc.sbuf_tensor("wuT2", [128, NDC, II], BF16))
            wdT = esw.enter_context(nc.sbuf_tensor("wdT2", [128, NIC, D], BF16))
            with ExitStack() as es:
                ph = Ph()
                ph.wait("sync", [(smisc, sm["stash_done"])])
                ph.on("sync", lambda e: e.dma_start(
                    out=wgT[:], in_=wg_st.rearrange("(k p) i -> p k i", p=128)),
                    inc=smisc, inc_by=16)
                ph.on("sync", lambda e: e.dma_start(
                    out=wuT[:], in_=wu_st.rearrange("(k p) i -> p k i", p=128)),
                    inc=smisc, inc_by=16)
                ph.on("sync", lambda e: e.dma_start(
                    out=wdT[:], in_=wd_st.rearrange("(k p) i -> p k i", p=128)),
                    inc=smisc, inc_by=16)
                fb2 = mk_fin_bufs(nc, es, "g_", NT)
                fv2 = emit_stats_final(ph, sm, fb2, sums2, sqs2, m2_all,
                                       veps2_all, NT, sv.v)
                emit_rstd(ph, sm, veps2_all, rstd2_all, scr_all, 0, NT, fv2)
                ph.wait("tensor", [(smisc, smisc.v)])
                ph.run(nc)
            build_ffn_phase(nc, sm, resid2, out_d, wgT[:], wuT[:], wdT[:],
                            m2_all, rstd2_all, identb, "f3_", out_stats=None)

        with ExitStack() as es:
            ph = Ph()
            ph.wait("gpsimd", [(sstp[0], sstp[0].v), (sstp[1], sstp[1].v),
                               (smisc, smisc.v)])
            ph.run(nc)

    return nc


_NC_CACHE = {}


def _get_nc(debug=False):
    if debug not in _NC_CACHE:
        _NC_CACHE[debug] = build_program(debug)
    return _NC_CACHE[debug]


def make_in_maps(L0_attn_x, L0_attn_Wz, L0_attn_bz, L0_attn_Wh, L0_attn_bh,
                 L0_attn_Wg, L0_attn_bg, L0_ffn_w_gate, L0_ffn_w_up,
                 L0_ffn_w_down):
    x = np.ascontiguousarray(np.asarray(L0_attn_x, dtype=np.float32))
    wz2 = np.ascontiguousarray(
        np.transpose(np.asarray(L0_attn_Wz, np.float32), (1, 0, 2)).reshape(D, D))
    wh2 = np.ascontiguousarray(
        np.transpose(np.asarray(L0_attn_Wh, np.float32), (1, 0, 2)).reshape(D, D))
    wgd = np.ascontiguousarray(np.asarray(L0_attn_Wg, np.float32))
    bzf = np.ascontiguousarray(np.asarray(L0_attn_bz, np.float32).reshape(D))
    bhf = np.ascontiguousarray(np.asarray(L0_attn_bh, np.float32).reshape(D))
    bgf = np.ascontiguousarray(np.asarray(L0_attn_bg, np.float32).reshape(D))
    wgate = np.ascontiguousarray(np.asarray(L0_ffn_w_gate, np.float32))
    wup = np.ascontiguousarray(np.asarray(L0_ffn_w_up, np.float32))
    wdown = np.ascontiguousarray(np.asarray(L0_ffn_w_down, np.float32))
    identb = np.eye(128, dtype=ml_dtypes.bfloat16)
    identf = np.eye(128, dtype=np.float32)
    shards = x.reshape(NCORES, TOK, D)
    return [{"x_in": np.ascontiguousarray(shards[i]), "wz2": wz2, "wh2": wh2,
             "wgd": wgd, "bzf": bzf, "bhf": bhf, "bgf": bgf, "wgate": wgate,
             "wup": wup, "wdown": wdown, "identb_d": identb,
             "identf_d": identf} for i in range(NCORES)]


def kernel(**inputs):
    nc = _get_nc(False)
    in_maps = make_in_maps(**inputs)
    res = run_bass_kernel_spmd(nc, in_maps, core_ids=list(range(NCORES)))
    out = np.stack([np.asarray(res.results[i]["out"]) for i in range(NCORES)], 0)
    return out.reshape(B, S, D).astype(np.float32)


# version 13
# speedup vs baseline: 1.1123x; 1.0116x over previous
import numpy as np
import ml_dtypes
from contextlib import ExitStack

import concourse.bass as bass
import concourse.mybir as mybir
from concourse.bass_utils import run_bass_kernel_spmd

F32 = mybir.dt.float32
BF16 = mybir.dt.bfloat16
ALU = mybir.AluOpType
ACT = mybir.ActivationFunctionType
AXX = mybir.AxisListType.X

B, S, D = 32, 2048, 1024
NCORES = 8
SEQ_PC = B // NCORES
TOK = SEQ_PC * S
NT = TOK // 128
TPS = S // 128
II = 2816
NIC = II // 128
NDC = D // 128
IHALF = II // 2
NH = IHALF // 128
EPS = 1e-5

_TH = (0.25, 0.75, 1.25, 1.75, 2.5, 3.5, 5.0)
TH2 = tuple(t * t for t in _TH)
WQ = (0.5, 0.5, 0.5, 0.5, 1.0, 1.0, 2.0)

ENGS = ("sync", "tensor", "vector", "scalar", "gpsimd")
import os as _os
DBG_SEQ = int(_os.environ.get("KDBG_SEQ", "0"))


class Sem:
    __slots__ = ("h", "v", "name")

    def __init__(self, h, name):
        self.h = h
        self.v = 0
        self.name = name


class Ph:

    def __init__(self):
        self.ops = {e: [] for e in ENGS}

    def on(self, eng, emit, waits=(), inc=None, inc_by=1):
        if inc is not None:
            inc.v += inc_by

        def f(e, emit=emit, waits=tuple(waits), inc=inc, inc_by=inc_by):
            for (s, v) in waits:
                if v > 0:
                    e.wait_ge(s.h, v)
            ins = emit(e)
            if inc is not None:
                ins.then_inc(inc.h, inc_by)
        self.ops[eng].append(f)

    def wait(self, eng, waits):
        def f(e, waits=tuple(waits)):
            for (s, v) in waits:
                if v > 0:
                    e.wait_ge(s.h, v)
        self.ops[eng].append(f)

    def run(self, nc):
        with nc.Block() as block:
            @block.sync
            def _(e):
                for f in self.ops["sync"]:
                    f(e)

            @block.tensor
            def _(e):
                for f in self.ops["tensor"]:
                    f(e)

            @block.vector
            def _(e):
                for f in self.ops["vector"]:
                    f(e)

            @block.scalar
            def _(e):
                for f in self.ops["scalar"]:
                    f(e)

            @block.gpsimd
            def _(e):
                for f in self.ops["gpsimd"]:
                    f(e)


def _b16(t, nb):
    return t[:, 0:nb].unsqueeze(2).broadcast_to([128, nb, 16])


def mk_qdq_bufs(nc, es, prefix, width):
    qb = {}
    for n in ("fv", "fv2"):
        qb[n] = es.enter_context(nc.sbuf_tensor(prefix + n, [128, width], F32))
    for n in ("cA", "cB", "t1", "t2", "t3", "sgn"):
        qb[n] = es.enter_context(nc.sbuf_tensor(prefix + n, [128, width], BF16))
    qb["thb"] = es.enter_context(
        nc.sbuf_tensor(prefix + "thb", [128, 5], F32))
    for n in ("bm", "sc", "rec", "rc2"):
        qb[n] = es.enter_context(nc.sbuf_tensor(prefix + n, [128, width // 16], F32))
    return qb


def emit_qdq_init(ph, qb):
    ph.on("vector", lambda e: e.memset(qb["thb"][:, 0:1], -TH2[4]))
    ph.on("vector", lambda e: e.memset(qb["thb"][:, 1:2], -TH2[5]))
    ph.on("vector", lambda e: e.memset(qb["thb"][:, 2:3], -TH2[2]))
    ph.on("vector", lambda e: e.memset(qb["thb"][:, 3:4], -TH2[3]))
    ph.on("vector", lambda e: e.memset(qb["thb"][:, 4:5], -TH2[6]),
          inc=None)


def emit_qdq(ph, sm, qb, src, sign_src, out_bf16, width, fold=1.0):
    nb = width // 16
    V, Ssem = sm["sv"], sm["ss"]
    fv, fv2 = qb["fv"], qb["fv2"]
    cA, cB, t1, t2, t3 = qb["cA"], qb["cB"], qb["t1"], qb["t2"], qb["t3"]
    sg45 = fv[:].bitcast(BF16)
    bm, sc, rec, rc2 = qb["bm"], qb["sc"], qb["rec"], qb["rc2"]
    src3 = src.rearrange("p (nb b) -> p nb b", b=16)
    ph.on("vector", lambda e: e.tensor_reduce(
        out=bm[:, 0:nb], in_=src3, axis=AXX, op=ALU.max,
        apply_absolute_value=True))
    ph.on("vector", lambda e: e.tensor_scalar(
        out=sc[:, 0:nb], in0=bm[:, 0:nb], scalar1=6e-12, scalar2=fold / 6.0,
        op0=ALU.max, op1=ALU.mult))
    ph.on("vector", lambda e: e.tensor_scalar(
        out=rc2[:, 0:nb], in0=bm[:, 0:nb], scalar1=6e-12, scalar2=1.0 / 6.0,
        op0=ALU.max, op1=ALU.mult))
    ph.on("vector", lambda e: e.reciprocal(out=rec[:, 0:nb], in_=rc2[:, 0:nb]))
    rec_b = _b16(rec, nb)
    sc_b = _b16(sc, nb)
    fvv = fv[:, 0:width]
    fv2v = fv2[:, 0:width]
    ph.on("vector", lambda e: e.tensor_tensor(
        out=fvv.rearrange("p (nb b) -> p nb b", b=16), in0=src3, in1=rec_b,
        op=ALU.mult), inc=V)
    v_ready = V.v
    ph.on("vector", lambda e: e.tensor_tensor(
        out=fv2v, in0=fvv, in1=fvv, op=ALU.mult), inc=V)
    v2_ready = V.v
    ph.on("scalar", lambda e: e.activation(
        out=qb["sgn"][:, 0:width], in_=sign_src, func=ACT.Sign),
        waits=[(V, v_ready)], inc=Ssem)
    ph.on("scalar", lambda e: e.activation(
        out=t2[:, 0:width], in_=fv2v, func=ACT.Sign,
        bias=qb["thb"][:, 2:3]), waits=[(V, v2_ready)], inc=Ssem)
    ph.on("scalar", lambda e: e.activation(
        out=t3[:, 0:width], in_=fv2v, func=ACT.Sign,
        bias=qb["thb"][:, 3:4]), inc=Ssem)
    s23_done = Ssem.v
    ph.on("scalar", lambda e: e.activation(
        out=sg45[:, 0:width], in_=fv2v, func=ACT.Sign,
        bias=qb["thb"][:, 0:1]), inc=Ssem)
    ph.on("scalar", lambda e: e.activation(
        out=sg45[:, width:2 * width], in_=fv2v, func=ACT.Sign,
        bias=qb["thb"][:, 1:2]), inc=Ssem)
    s_done = Ssem.v

    def cmp(dst, i):
        ph.on("vector", lambda e, i=i, dst=dst: e.tensor_scalar(
            out=dst[:, 0:width], in0=fv2v, scalar1=TH2[i], scalar2=WQ[i],
            op0=ALU.is_gt, op1=ALU.mult))

    def add(dst, a, b, w=(), inc=None):
        ph.on("vector", lambda e, dst=dst, a=a, b=b: e.tensor_tensor(
            out=dst[:, 0:width], in0=a[:, 0:width], in1=b[:, 0:width],
            op=ALU.add), waits=w, inc=inc)

    cmp(cA, 0); cmp(cB, 1); add(t1, cA, cB)
    add(cA, t2, t3, w=[(Ssem, s23_done)], inc=V)
    tB_done = V.v
    ph.on("scalar", lambda e: e.activation(
        out=t2[:, 0:width], in_=fv2v, func=ACT.Sign,
        bias=qb["thb"][:, 4:5]), waits=[(V, tB_done)], inc=Ssem)
    s6_done = Ssem.v
    ph.on("vector", lambda e: e.scalar_tensor_tensor(
        out=cB[:, 0:width], in0=cA[:, 0:width], scalar=0.25,
        in1=t1[:, 0:width], op0=ALU.mult, op1=ALU.add))
    add(t1, sg45[:, 0:width], sg45[:, width:2 * width],
        w=[(Ssem, s_done)])
    ph.on("vector", lambda e: e.tensor_scalar(
        out=cA[:, 0:width], in0=t1[:, 0:width], scalar1=0.5, scalar2=2.5,
        op0=ALU.mult, op1=ALU.add))
    add(t3, cA, t2, w=[(Ssem, s6_done)])
    add(cA, cB, t3)
    ph.on("vector", lambda e: e.tensor_tensor(
        out=cB[:, 0:width], in0=cA[:, 0:width], in1=qb["sgn"][:, 0:width],
        op=ALU.mult))
    ph.on("vector", lambda e: e.tensor_tensor(
        out=out_bf16.rearrange("p (nb b) -> p nb b", b=16),
        in0=cB[:, 0:width].rearrange("p (nb b) -> p nb b", b=16), in1=sc_b,
        op=ALU.mult), inc=V)
    return V.v


def emit_tile_stats(ph, sm, x2scr, sums_all, sqs_all, t, src, wait_src=()):
    V, Ssem = sm["sv"], sm["ss"]
    ph.on("scalar", lambda e: e.activation(
        out=x2scr[:, 0:D], in_=src, func=ACT.Square),
        waits=list(wait_src) + [(V, V.v)], inc=Ssem)
    x2_done = Ssem.v
    ph.on("vector", lambda e: e.tensor_reduce(
        out=sums_all[:, t:t + 1], in_=src, axis=AXX, op=ALU.add),
        waits=wait_src)
    ph.on("vector", lambda e: e.tensor_reduce(
        out=sqs_all[:, t:t + 1], in_=x2scr[:, 0:D], axis=AXX, op=ALU.add),
        waits=[(Ssem, x2_done)], inc=V)
    return V.v


def mk_fin_bufs(nc, es, prefix, nt):
    fb = {}
    for n in ("p2", "vt"):
        fb[n] = es.enter_context(nc.sbuf_tensor(prefix + n, [128, nt], F32))
    return fb


def emit_stats_final(ph, sm, fb, sums_all, sqs_all, m_all, veps_all, nt, wait_v):
    V = sm["sv"]
    ph.on("vector", lambda e: e.tensor_scalar(
        out=m_all[:, 0:nt], in0=sums_all[:, 0:nt], scalar1=1.0 / 1024.0,
        scalar2=None, op0=ALU.mult), waits=[(V, wait_v)])
    ph.on("vector", lambda e: e.tensor_tensor(
        out=fb["p2"][:, 0:nt], in0=m_all[:, 0:nt], in1=m_all[:, 0:nt],
        op=ALU.mult))
    ph.on("vector", lambda e: e.tensor_scalar(
        out=fb["vt"][:, 0:nt], in0=sqs_all[:, 0:nt], scalar1=1.0 / 1024.0,
        scalar2=EPS, op0=ALU.mult, op1=ALU.add))
    ph.on("vector", lambda e: e.tensor_tensor(
        out=veps_all[:, 0:nt], in0=fb["vt"][:, 0:nt], in1=fb["p2"][:, 0:nt],
        op=ALU.subtract), inc=V)
    return V.v


def emit_rstd(ph, sm, veps_all, rstd_all, scr, lo, n, wait_v):
    V, Ssem = sm["sv"], sm["ss"]
    ph.on("scalar", lambda e: e.activation(
        out=scr[:, lo:lo + n], in_=veps_all[:, lo:lo + n], func=ACT.Sqrt),
        waits=[(V, wait_v)], inc=Ssem)
    sq = Ssem.v
    ph.on("vector", lambda e: e.reciprocal(
        out=rstd_all[:, lo:lo + n], in_=scr[:, lo:lo + n]),
        waits=[(Ssem, sq)], inc=V)
    return V.v


def build_ffn_phase(nc, sm, src_dram, dst_dram, wgT, wuT, wdT, m_all, rstd_all,
                    identb, tag, out_stats=None, pre_waits=(), dbg=None):
    sldp, sstp, sv, ss, st_ = sm["sldp"], sm["sstp"], sm["sv"], sm["ss"], sm["st"]
    with ExitStack() as es:
        ph = Ph()
        for eng in ENGS:
            ph.wait(eng, pre_waits)
        sb = lambda n, shp, dt: es.enter_context(nc.sbuf_tensor(tag + n, shp, dt))
        xin = [sb("f_xin0", [128, D], F32), sb("f_xin1", [128, D], F32)]
        rout = [sb("f_rout0", [128, D], F32), sb("f_rout1", [128, D], F32)]
        x_ln = sb("f_xln", [128, D], F32)
        xq = [sb("f_xq0", [128, D], BF16)] * 2
        xqT = [sb("f_xqT0", [128, NDC, 128], BF16)] * 2
        qb = mk_qdq_bufs(nc, es, tag + "q_", IHALF)
        silu = [sb("f_silu0", [128, 512], F32), sb("f_silu1", [128, 512], F32)]
        hhalf = sb("f_hh", [128, IHALF], F32)
        hq = sb("f_hq", [128, IHALF], BF16)
        hqT = sb("f_hqT", [128, NH, 128], BF16)


        p_tr = es.enter_context(nc.psum_tensor(tag + "ptr", [128, 8, 128], BF16))
        p_tr2 = es.enter_context(nc.psum_tensor(tag + "ptr2", [128, 8, 128], BF16))
        p_g = [es.enter_context(nc.psum_tensor(f"{tag}pg{i}", [128, 512], F32))
               for i in range(2)]
        p_u = [es.enter_context(nc.psum_tensor(f"{tag}pu{i}", [128, 512], F32))
               for i in range(2)]
        p_d = [es.enter_context(nc.psum_tensor(f"{tag}pd{i}", [128, 512], F32))
               for i in range(2)]

        emit_qdq_init(ph, qb)
        slabs = [(0, 512), (512, 512), (1024, 384)]
        last_xin_free = [0, 0]
        last_store = [0, 0]
        last_xq_use = [0, 0]
        last_xqT_use = [0, 0]
        last_tr_evac = 0
        last_tr2_evac = 0
        last_gu_free = [[0, 0], [0, 0]]
        last_hqT_use = 0
        last_pd_free = 0
        gslab = 0

        for t in range(NT):
            p = t % 2
            ph.on("sync", lambda e, t=t, p=p: e.dma_start(
                out=xin[p][:], in_=src_dram[t * 128:(t + 1) * 128, :]),
                waits=[(sv, last_xin_free[p])], inc=sldp[p], inc_by=16)
            ld_here = sldp[p].v
            ph.on("vector", lambda e, t=t, p=p: e.tensor_scalar(
                out=x_ln[:], in0=xin[p][:], scalar1=m_all[:, t:t + 1],
                scalar2=rstd_all[:, t:t + 1], op0=ALU.subtract, op1=ALU.mult),
                waits=[(sldp[p], ld_here)], inc=sv)
            ph.wait("vector", [(st_, last_xq_use[p])])
            if dbg is not None and t == 0:
                xln_v = sv.v
                ph.on("gpsimd", lambda e: e.dma_start(
                    out=dbg["xln"][:], in_=x_ln[:]), waits=[(sv, xln_v)],
                    inc=sstp[0], inc_by=16)
                ph.wait("vector", [(sstp[0], sstp[0].v)])
            xq_ready = emit_qdq(ph, sm, qb, x_ln[:], x_ln[:], xq[p][:], D)
            if dbg is not None and t == 0:
                ph.on("gpsimd", lambda e, p=p: e.dma_start(
                    out=dbg["xq"][:], in_=xq[p][:]), waits=[(sv, xq_ready)],
                    inc=sstp[0], inc_by=16)
                ph.wait("vector", [(sstp[0], sstp[0].v)])
            for k in range(NDC):
                ph.on("tensor", lambda e, k=k, p=p: e.transpose(
                    out=p_tr[:, k, :], in_=xq[p][:, k * 128:(k + 1) * 128],
                    identity=identb[:]),
                    waits=([(sv, xq_ready), (ss, last_tr_evac)] if k == 0 else ()),
                    inc=(st_ if k == NDC - 1 else None))
            last_xq_use[0] = last_xq_use[1] = st_.v
            trx_done = st_.v
            ph.on("scalar", lambda e, p=p: e.copy(out=xqT[p][:], in_=p_tr[:]),
                  waits=[(st_, trx_done), (st_, last_xqT_use[p])], inc=ss)
            last_tr_evac = ss.v
            xqT_ready = ss.v

            for half in range(2):
                ioff = half * IHALF
                for (soff, n) in slabs:
                    bp = gslab % 2
                    for k in range(NDC):
                        ph.on("tensor", lambda e, k=k, p=p, soff=soff, n=n, bp=bp, ioff=ioff: e.matmul(
                            out=p_g[bp][:, 0:n], lhsT=xqT[p][:, k, :],
                            rhs=wgT[:, k, ioff + soff:ioff + soff + n],
                            start=(k == 0), stop=(k == NDC - 1)),
                            waits=([(ss, xqT_ready), (sv, last_gu_free[0][bp])]
                                   if k == 0 else ()),
                            inc=(st_ if k == NDC - 1 else None))
                    g_done = st_.v
                    for k in range(NDC):
                        ph.on("tensor", lambda e, k=k, p=p, soff=soff, n=n, bp=bp, ioff=ioff: e.matmul(
                            out=p_u[bp][:, 0:n], lhsT=xqT[p][:, k, :],
                            rhs=wuT[:, k, ioff + soff:ioff + soff + n],
                            start=(k == 0), stop=(k == NDC - 1)),
                            waits=([(sv, last_gu_free[1][bp])] if k == 0 else ()),
                            inc=(st_ if k == NDC - 1 else None))
                    u_done = st_.v
                    ph.on("scalar", lambda e, bp=bp, n=n: e.activation(
                        out=silu[bp][:, 0:n], in_=p_g[bp][:, 0:n], func=ACT.Silu),
                        waits=[(st_, g_done)], inc=ss)
                    silu_done = ss.v
                    ph.on("vector", lambda e, bp=bp, n=n, soff=soff: e.tensor_tensor(
                        out=hhalf[:, soff:soff + n], in0=silu[bp][:, 0:n],
                        in1=p_u[bp][:, 0:n], op=ALU.mult),
                        waits=[(ss, silu_done), (st_, u_done)], inc=sv)
                    last_gu_free[0][bp] = sv.v
                    last_gu_free[1][bp] = sv.v
                    gslab += 1
                    if half == 1:
                        last_xqT_use[0] = last_xqT_use[1] = u_done
                ph.wait("vector", [(st_, last_hqT_use)])
                if dbg is not None and t == 0 and half == 0:
                    hh_v = sv.v
                    ph.on("gpsimd", lambda e: e.dma_start(
                        out=dbg["hh"][:], in_=hhalf[:]), waits=[(sv, hh_v)],
                        inc=sstp[0], inc_by=16)
                    ph.wait("vector", [(sstp[0], sstp[0].v)])
                hq_ready = emit_qdq(ph, sm, qb, hhalf[:], hhalf[:], hq[:], IHALF)
                if dbg is not None and t == 0 and half == 0:
                    ph.on("gpsimd", lambda e: e.dma_start(
                        out=dbg["hq"][:], in_=hq[:]), waits=[(sv, hq_ready)],
                        inc=sstp[0], inc_by=16)
                    ph.wait("vector", [(sstp[0], sstp[0].v)])
                for j in range(NH):
                    tgt = p_tr[:, j, :] if j < 8 else p_tr2[:, j - 8, :]
                    w0 = [(sv, hq_ready), (ss, last_tr_evac)]
                    w8 = [(ss, last_tr2_evac)]
                    ph.on("tensor", lambda e, j=j, tgt=tgt: e.transpose(
                        out=tgt, in_=hq[:, j * 128:(j + 1) * 128],
                        identity=identb[:]),
                        waits=(w0 if j == 0 else (w8 if j == 8 else ())),
                        inc=(st_ if j == NH - 1 else None))
                trh_done = st_.v
                ph.on("scalar", lambda e: e.copy(
                    out=hqT[:, 0:8, :], in_=p_tr[:]),
                    waits=[(st_, trh_done), (st_, last_hqT_use)], inc=ss)
                last_tr_evac = ss.v
                ph.on("scalar", lambda e: e.copy(
                    out=hqT[:, 8:NH, :], in_=p_tr2[:, 0:3, :]), inc=ss)
                last_tr2_evac = ss.v
                hqT_ready = ss.v
                for dslab in range(2):
                    for j in range(NH):
                        w = []
                        if j == 0:
                            w = [(ss, hqT_ready)]
                            if half == 0 and dslab == 0:
                                w.append((sv, last_pd_free))
                        ph.on("tensor", lambda e, j=j, dslab=dslab, half=half: e.matmul(
                            out=p_d[dslab][:], lhsT=hqT[:, j, :],
                            rhs=wdT[:, half * NH + j,
                                    dslab * 512:(dslab + 1) * 512],
                            start=(half == 0 and j == 0),
                            stop=(half == 1 and j == NH - 1)),
                            waits=w,
                            inc=(st_ if (half == 1 and j == NH - 1) else None))
                last_hqT_use = st_.v
            down_done = st_.v
            ph.wait("vector", [(sstp[p], last_store[p])])
            for dslab in range(2):
                ph.on("vector", lambda e, dslab=dslab, p=p: e.tensor_tensor(
                    out=rout[p][:, dslab * 512:(dslab + 1) * 512],
                    in0=p_d[dslab][:],
                    in1=xin[p][:, dslab * 512:(dslab + 1) * 512], op=ALU.add),
                    waits=([(st_, down_done)] if dslab == 0 else ()), inc=sv)
            last_xin_free[p] = sv.v
            last_pd_free = sv.v
            resid_done = sv.v
            if out_stats is not None:
                o_sums, o_sqs = out_stats
                resid_done = emit_tile_stats(ph, sm, qb["fv2"], o_sums, o_sqs,
                                             t, rout[p][:],
                                             wait_src=[(sv, resid_done)])
            ph.on("gpsimd", lambda e, t=t, p=p: e.dma_start(
                out=dst_dram[t * 128:(t + 1) * 128, :], in_=rout[p][:]),
                waits=[(sv, resid_done)], inc=sstp[p], inc_by=16)
            last_store[p] = sstp[p].v
        ph.wait("gpsimd", [(sstp[0], sstp[0].v), (sstp[1], sstp[1].v)])
        ph.run(nc)


def build_program(debug=False):
    nc = bass.Bass("TRN2", target_bir_lowering=False)

    x_in = nc.dram_tensor("x_in", [TOK, D], F32, kind="ExternalInput").ap()
    wz2 = nc.dram_tensor("wz2", [D, D], F32, kind="ExternalInput").ap()
    wh2 = nc.dram_tensor("wh2", [D, D], F32, kind="ExternalInput").ap()
    wgd = nc.dram_tensor("wgd", [D, D], F32, kind="ExternalInput").ap()
    bzf = nc.dram_tensor("bzf", [D], F32, kind="ExternalInput").ap()
    bhf = nc.dram_tensor("bhf", [D], F32, kind="ExternalInput").ap()
    bgf = nc.dram_tensor("bgf", [D], F32, kind="ExternalInput").ap()
    wgate = nc.dram_tensor("wgate", [II, D], F32, kind="ExternalInput").ap()
    wup = nc.dram_tensor("wup", [II, D], F32, kind="ExternalInput").ap()
    wdown = nc.dram_tensor("wdown", [D, II], F32, kind="ExternalInput").ap()
    identb_d = nc.dram_tensor("identb_d", [128, 128], BF16, kind="ExternalInput").ap()
    identf_d = nc.dram_tensor("identf_d", [128, 128], F32, kind="ExternalInput").ap()
    out_d = nc.dram_tensor("out", [TOK, D], F32, kind="ExternalOutput").ap()
    kres = "ExternalOutput" if debug else "Internal"
    resid1 = nc.dram_tensor("resid1", [TOK, D], F32, kind=kres).ap()
    resid2 = nc.dram_tensor("resid2", [TOK, D], F32, kind=kres).ap()
    wg_st = nc.dram_tensor("wg_st", [NDC * 128, II], BF16, kind="Internal").ap()
    dbg = None
    if debug:
        dbg = {
            "xln": nc.dram_tensor("dbg_xln", [128, D], F32, kind="ExternalOutput").ap(),
            "xq": nc.dram_tensor("dbg_xq", [128, D], BF16, kind="ExternalOutput").ap(),
            "hh": nc.dram_tensor("dbg_hh", [128, IHALF], F32, kind="ExternalOutput").ap(),
            "hq": nc.dram_tensor("dbg_hq", [128, IHALF], BF16, kind="ExternalOutput").ap(),
            "wgT": nc.dram_tensor("dbg_wgT", [128, NDC, II], BF16, kind="ExternalOutput").ap(),
            "wdT": nc.dram_tensor("dbg_wdT", [128, NIC, D], BF16, kind="ExternalOutput").ap(),
            "mx": nc.dram_tensor("dbg_mx", [128, NT], F32, kind="ExternalOutput").ap(),
            "rstdx": nc.dram_tensor("dbg_rstdx", [128, NT], F32, kind="ExternalOutput").ap(),
            "wzb": nc.dram_tensor("dbg_wzb", [128, NDC, D], BF16, kind="ExternalOutput").ap(),
            "xlnT": nc.dram_tensor("dbg_xlnT", [128, NDC, S], BF16, kind="ExternalOutput").ap(),
            "abuf": nc.dram_tensor("dbg_abuf", [128, S], F32, kind="ExternalOutput").ap(),
            "bbuf": nc.dram_tensor("dbg_bbuf", [128, S], F32, kind="ExternalOutput").ap(),
            "h0": nc.dram_tensor("dbg_h0", [128, S], F32, kind="ExternalOutput").ap(),
            "hall": nc.dram_tensor("dbg_hall", [128, NDC, S], F32, kind="ExternalOutput").ap(),
            "rst1": nc.dram_tensor("dbg_rst1", [128, NT], F32, kind="ExternalOutput").ap(),
            "hb": nc.dram_tensor("dbg_hb", [128, NDC, 512], BF16, kind="ExternalOutput").ap(),
            "g": nc.dram_tensor("dbg_g", [128, NDC, 512], F32, kind="ExternalOutput").ap(),
            "ao": nc.dram_tensor("dbg_ao", [128, NDC, 512], F32, kind="ExternalOutput").ap(),
            "taoT": nc.dram_tensor("dbg_taoT", [128, 2, 4, 128], F32, kind="ExternalOutput").ap(),
            "wzb2": nc.dram_tensor("dbg_wzb2", [128, NDC, D], BF16, kind="ExternalOutput").ap(),
            "wgb2": nc.dram_tensor("dbg_wgb2", [128, NDC, D], BF16, kind="ExternalOutput").ap(),
            "pgt0": nc.dram_tensor("dbg_pgt0", [128, 512], F32, kind="ExternalOutput").ap(),
            "pgt1": nc.dram_tensor("dbg_pgt1", [128, 512], F32, kind="ExternalOutput").ap(),
            "wgb1": nc.dram_tensor("dbg_wgb1", [128, NDC, D], BF16, kind="ExternalOutput").ap(),
            "wgb15": nc.dram_tensor("dbg_wgb15", [128, NDC, D], BF16, kind="ExternalOutput").ap(),
            "bzs": nc.dram_tensor("dbg_bzs", [128, NDC], F32, kind="ExternalOutput").ap(),
            "st1": nc.dram_tensor("dbg_st1", [128, NT * 12], F32, kind="ExternalOutput").ap(),
            "m1": nc.dram_tensor("dbg_m1", [128, NT], F32, kind="ExternalOutput").ap(),
            "v1": nc.dram_tensor("dbg_v1", [128, NT], F32, kind="ExternalOutput").ap(),
            "fsq": nc.dram_tensor("dbg_fsq", [128, NT, 4], F32, kind="ExternalOutput").ap(),
            "fms": nc.dram_tensor("dbg_fms", [128, NT], F32, kind="ExternalOutput").ap(),
        }
    wu_st = nc.dram_tensor("wu_st", [NDC * 128, II], BF16, kind="Internal").ap()
    wd_st = nc.dram_tensor("wd_st", [NIC * 128, D], BF16, kind="Internal").ap()

    prog = ExitStack()
    with prog:
        sm = {}
        for n in ("sld0", "sld1", "sst0", "sst1", "smisc", "sv", "ss", "st"):
            sm[n] = Sem(prog.enter_context(nc.semaphore(n)), n)
        sldp = [sm["sld0"], sm["sld1"]]
        sstp = [sm["sst0"], sm["sst1"]]
        smisc = sm["smisc"]
        sm["sldp"] = sldp
        sm["sstp"] = sstp
        sv, ss, st_ = sm["sv"], sm["ss"], sm["st"]

        identb = prog.enter_context(nc.sbuf_tensor("identb_s", [128, 128], BF16))
        identf = prog.enter_context(nc.sbuf_tensor("identf_s", [128, 128], F32))
        m1_all = prog.enter_context(nc.sbuf_tensor("m1_all", [128, NT], F32))
        veps1_all = prog.enter_context(nc.sbuf_tensor("veps1_all", [128, NT], F32))
        rstd1_all = prog.enter_context(nc.sbuf_tensor("rstd1_all", [128, NT], F32))
        m2_all = prog.enter_context(nc.sbuf_tensor("m2_all", [128, NT], F32))
        veps2_all = prog.enter_context(nc.sbuf_tensor("veps2_all", [128, NT], F32))
        rstd2_all = prog.enter_context(nc.sbuf_tensor("rstd2_all", [128, NT], F32))
        mx_all = prog.enter_context(nc.sbuf_tensor("mx_all", [128, NT], F32))
        vepsx_all = prog.enter_context(nc.sbuf_tensor("vepsx_all", [128, NT], F32))
        rstdx_all = prog.enter_context(nc.sbuf_tensor("rstdx_all", [128, NT], F32))
        scr_all = prog.enter_context(nc.sbuf_tensor("scr_all", [128, NT], F32))
        sums1 = prog.enter_context(nc.sbuf_tensor("sums1", [128, NT], F32))
        sqs1 = prog.enter_context(nc.sbuf_tensor("sqs1", [128, NT], F32))
        sums2 = prog.enter_context(nc.sbuf_tensor("sums2", [128, NT], F32))
        sqs2 = prog.enter_context(nc.sbuf_tensor("sqs2", [128, NT], F32))

        with ExitStack() as es:
            ph = Ph()
            ph.on("sync", lambda e: e.dma_start(out=identb[:], in_=identb_d[:]),
                  inc=smisc, inc_by=16)
            ph.on("sync", lambda e: e.dma_start(out=identf[:], in_=identf_d[:]),
                  inc=smisc, inc_by=16)
            xa = [es.enter_context(nc.sbuf_tensor(f"pa_x{i}", [128, D], F32))
                  for i in range(2)]
            sumsx = es.enter_context(nc.sbuf_tensor("sumsx", [128, NT], F32))
            sqsx = es.enter_context(nc.sbuf_tensor("sqsx", [128, NT], F32))
            x2scr = es.enter_context(nc.sbuf_tensor("pa_x2", [128, D], F32))
            fb = mk_fin_bufs(nc, es, "pa_", NT)
            done_v = [0, 0]
            for t in range(NT):
                p = t % 2
                ph.on("sync", lambda e, t=t, p=p: e.dma_start(
                    out=xa[p][:], in_=x_in[t * 128:(t + 1) * 128, :]),
                    waits=[(sv, done_v[p])], inc=sldp[p], inc_by=16)
                done_v[p] = emit_tile_stats(ph, sm, x2scr, sumsx, sqsx, t,
                                            xa[p][:],
                                            wait_src=[(sldp[p], sldp[p].v)])
            fv_ = emit_stats_final(ph, sm, fb, sumsx, sqsx, mx_all, vepsx_all,
                                   NT, sv.v)
            rv = emit_rstd(ph, sm, vepsx_all, rstdx_all, scr_all, 0, NT, fv_)
            if dbg is not None:
                ph.on("gpsimd", lambda e: e.dma_start(
                    out=dbg["mx"][:], in_=mx_all[:]), waits=[(sv, rv)],
                    inc=sstp[0], inc_by=16)
                ph.on("gpsimd", lambda e: e.dma_start(
                    out=dbg["rstdx"][:], in_=rstdx_all[:]), inc=sstp[0], inc_by=16)
            ph.wait("gpsimd", [(sstp[0], sstp[0].v)])
            ph.run(nc)

        with ExitStack() as esw:
            wgT = esw.enter_context(nc.sbuf_tensor("wgT", [128, NDC, II], BF16))
            wuT = esw.enter_context(nc.sbuf_tensor("wuT", [128, NDC, II], BF16))
            wdT = esw.enter_context(nc.sbuf_tensor("wdT", [128, NIC, D], BF16))

            with ExitStack() as es:
                ph = Ph()
                ph.wait("tensor", [(smisc, 32)])
                wst = [es.enter_context(nc.sbuf_tensor(f"wp_st{i}", [128, II], F32))
                       for i in range(2)]
                qb = mk_qdq_bufs(nc, es, "wp_", IHALF)
                wq = es.enter_context(nc.sbuf_tensor("wp_wq", [128, II], BF16))
                p_tr = es.enter_context(nc.psum_tensor("wp_ptr", [128, 8, 128], BF16))
                p_tr2 = es.enter_context(nc.psum_tensor("wp_ptr2", [128, 8, 128], BF16))
                emit_qdq_init(ph, qb)
                state = {"tr": 0, "tr2": 0, "wv": [0, 0], "trT": 0}

                def prep_gu(wdram, wT):
                    for c in range(NIC):
                        p = c % 2
                        ph.on("sync", lambda e, c=c, p=p, wdram=wdram: e.dma_start(
                            out=wst[p][:, 0:D], in_=wdram[c * 128:(c + 1) * 128, :]),
                            waits=[(sv, state["wv"][p])], inc=sldp[p], inc_by=16)
                        ph.wait("vector", [(sldp[p], sldp[p].v), (st_, state["trT"])])
                        q_ready = emit_qdq(ph, sm, qb, wst[p][:, 0:D],
                                           wst[p][:, 0:D], wq[:, 0:D], D)
                        state["wv"][p] = q_ready
                        for k in range(NDC):
                            ph.on("tensor", lambda e, k=k: e.transpose(
                                out=p_tr[:, k, :],
                                in_=wq[:, k * 128:(k + 1) * 128],
                                identity=identb[:]),
                                waits=([(sv, q_ready), (ss, state["tr"])]
                                       if k == 0 else ()),
                                inc=(st_ if k == NDC - 1 else None))
                        tr_done = st_.v
                        state["trT"] = st_.v
                        ph.on("scalar", lambda e, c=c, wT=wT: e.copy(
                            out=wT[:, 0:NDC, c * 128:(c + 1) * 128], in_=p_tr[:]),
                            waits=[(st_, tr_done)], inc=ss)
                        state["tr"] = ss.v

                def prep_down():
                    for c in range(NDC):
                        p = c % 2
                        ph.on("sync", lambda e, c=c, p=p: e.dma_start(
                            out=wst[p][:], in_=wdown[c * 128:(c + 1) * 128, :]),
                            waits=[(sv, state["wv"][p])], inc=sldp[p], inc_by=16)
                        ph.wait("vector", [(sldp[p], sldp[p].v), (st_, state["trT"])])
                        emit_qdq(ph, sm, qb, wst[p][:, 0:IHALF],
                                 wst[p][:, 0:IHALF], wq[:, 0:IHALF], IHALF,
                                 fold=0.5)
                        q_ready = emit_qdq(ph, sm, qb, wst[p][:, IHALF:II],
                                           wst[p][:, IHALF:II], wq[:, IHALF:II],
                                           IHALF, fold=0.5)
                        state["wv"][p] = q_ready
                        for grp, (j0, nj) in enumerate(((0, 8), (8, 8), (16, 6))):
                            ptr = p_tr if grp != 1 else p_tr2
                            trs = state["tr"] if grp != 1 else state["tr2"]
                            for j in range(nj):
                                ph.on("tensor", lambda e, j=j, j0=j0, ptr=ptr: e.transpose(
                                    out=ptr[:, j, :],
                                    in_=wq[:, (j0 + j) * 128:(j0 + j + 1) * 128],
                                    identity=identb[:]),
                                    waits=([(sv, q_ready), (ss, trs)]
                                           if j == 0 else ()),
                                    inc=(st_ if j == nj - 1 else None))
                            tr_done = st_.v
                            state["trT"] = st_.v
                            ph.on("scalar", lambda e, c=c, j0=j0, nj=nj, ptr=ptr: e.copy(
                                out=wdT[:, j0:j0 + nj, c * 128:(c + 1) * 128],
                                in_=ptr[:, 0:nj, :]),
                                waits=[(st_, tr_done)], inc=ss)
                            if grp != 1:
                                state["tr"] = ss.v
                            else:
                                state["tr2"] = ss.v

                prep_gu(wgate, wgT)
                prep_gu(wup, wuT)
                prep_down()
                wv = ss.v
                ph.on("gpsimd", lambda e: e.dma_start(
                    out=wg_st.rearrange("(k p) i -> p k i", p=128), in_=wgT[:]),
                    waits=[(ss, wv)], inc=smisc, inc_by=16)
                ph.on("gpsimd", lambda e: e.dma_start(
                    out=wu_st.rearrange("(k p) i -> p k i", p=128), in_=wuT[:]),
                    inc=smisc, inc_by=16)
                ph.on("gpsimd", lambda e: e.dma_start(
                    out=wd_st.rearrange("(k p) i -> p k i", p=128), in_=wdT[:]),
                    inc=smisc, inc_by=16)
                if dbg is not None:
                    wv2 = ss.v
                    ph.on("gpsimd", lambda e: e.dma_start(
                        out=dbg["wgT"][:], in_=wgT[:]), waits=[(ss, wv2)],
                        inc=smisc, inc_by=16)
                    ph.on("gpsimd", lambda e: e.dma_start(
                        out=dbg["wdT"][:], in_=wdT[:]), inc=smisc, inc_by=16)
                sm["stash_done"] = smisc.v
                ph.wait("gpsimd", [(smisc, smisc.v)])
                ph.run(nc)

            build_ffn_phase(nc, sm, x_in, resid1, wgT[:], wuT[:], wdT[:],
                            mx_all, rstdx_all, identb, "f1_",
                            out_stats=(sums1, sqs1), dbg=dbg)

        with ExitStack() as esa:
            wzb = esa.enter_context(nc.sbuf_tensor("wzb", [128, NDC, D], BF16))
            whb = esa.enter_context(nc.sbuf_tensor("whb", [128, NDC, D], BF16))
            wgb = esa.enter_context(nc.sbuf_tensor("wgb", [128, NDC, D], BF16))
            bz_s = esa.enter_context(nc.sbuf_tensor("bz_s", [128, NDC], F32))
            nbz_s = esa.enter_context(nc.sbuf_tensor("nbz_s", [128, NDC], F32))
            bh_s = esa.enter_context(nc.sbuf_tensor("bh_s", [128, NDC], F32))
            bg_s = esa.enter_context(nc.sbuf_tensor("bg_s", [128, NDC], F32))

            with ExitStack() as es:
                ph = Ph()
                wstg = [es.enter_context(nc.sbuf_tensor(f"aw{i}", [128, D], F32))
                        for i in range(2)]
                cnt = {"v": [0, 0]}
                for wi, (wsrc, wdst) in enumerate(((wz2, wzb), (wh2, whb), (wgd, wgb))):
                    for c in range(NDC):
                        p = (wi * NDC + c) % 2
                        ph.on("sync", lambda e, c=c, p=p, wsrc=wsrc: e.dma_start(
                            out=wstg[p][:], in_=wsrc[c * 128:(c + 1) * 128, :]),
                            waits=[(sv, cnt["v"][p])], inc=sldp[p], inc_by=16)
                        ph.on("vector", lambda e, c=c, p=p, wdst=wdst: e.tensor_copy(
                            out=wdst[:, c, :], in_=wstg[p][:]),
                            waits=[(sldp[p], sldp[p].v)], inc=sv)
                        cnt["v"][p] = sv.v
                for bsrc, bdst in ((bzf, bz_s), (bhf, bh_s), (bgf, bg_s)):
                    ph.on("sync", lambda e, bsrc=bsrc, bdst=bdst: e.dma_start(
                        out=bdst[:], in_=bsrc.rearrange("(c p) -> p c", p=128),
                        allow_slow_non_contiguous=True),
                        inc=smisc, inc_by=16)
                ph.on("vector", lambda e: e.tensor_scalar(
                    out=nbz_s[:], in0=bz_s[:], scalar1=-1.0, scalar2=None,
                    op0=ALU.mult), waits=[(smisc, smisc.v)], inc=sv)
                if dbg is not None:
                    wv3 = sv.v
                    ph.on("gpsimd", lambda e: e.dma_start(
                        out=dbg["st1"][:, 0:NT], in_=sums1[:]), waits=[(sv, wv3)],
                        inc=sstp[0], inc_by=16)
                    ph.on("gpsimd", lambda e: e.dma_start(
                        out=dbg["wzb"][:], in_=wzb[:]),
                        inc=sstp[0], inc_by=16)
                    ph.on("gpsimd", lambda e: e.dma_start(
                        out=dbg["wgb1"][:], in_=wgb[:]), inc=sstp[0], inc_by=16)
                    ph.on("gpsimd", lambda e: e.dma_start(
                        out=dbg["bzs"][:], in_=bz_s[:]), inc=sstp[0], inc_by=16)
                fb1 = mk_fin_bufs(nc, es, "d_", NT)
                fv1 = emit_stats_final(ph, sm, fb1, sums1, sqs1, m1_all,
                                       veps1_all, NT, sv.v)
                if dbg is not None:
                    ph.on("gpsimd", lambda e: e.dma_start(
                        out=dbg["m1"][:], in_=m1_all[:]), waits=[(sv, fv1)],
                        inc=sstp[0], inc_by=16)
                    ph.on("gpsimd", lambda e: e.dma_start(
                        out=dbg["v1"][:], in_=veps1_all[:]), inc=sstp[0], inc_by=16)
                    ph.on("gpsimd", lambda e: e.dma_start(
                        out=dbg["fms"][:], in_=fb1["p2"][:]), inc=sstp[0], inc_by=16)
                rrv = emit_rstd(ph, sm, veps1_all, rstd1_all, scr_all, 0, NT, fv1)
                if dbg is not None:
                    ph.on("gpsimd", lambda e: e.dma_start(
                        out=dbg["rst1"][:], in_=rstd1_all[:]), waits=[(sv, rrv)],
                        inc=sstp[0], inc_by=16)
                ph.wait("gpsimd", [(sstp[0], sstp[0].v)])
                ph.run(nc)

            for s in range(SEQ_PC):
                with ExitStack() as esq:
                    h_t = esq.enter_context(
                        nc.sbuf_tensor(f"a_h{s}", [128, NDC, S], F32))
                    with ExitStack() as es:
                        ph = Ph()
                        xlnT = es.enter_context(
                            nc.sbuf_tensor(f"a_xlnT{s}", [128, NDC, S], BF16))
                        xl_bf = es.enter_context(
                            nc.sbuf_tensor(f"a_xlbf{s}", [128, D], BF16))
                        rtile = [es.enter_context(
                            nc.sbuf_tensor(f"a_rt{s}_{i}", [128, D], F32))
                            for i in range(2)]
                        a_buf = es.enter_context(
                            nc.sbuf_tensor(f"a_a{s}", [128, S], F32))
                        bb_buf = es.enter_context(
                            nc.sbuf_tensor(f"a_bb{s}", [128, S], F32))
                        z_sb = [es.enter_context(
                            nc.sbuf_tensor(f"a_z{s}_{i}", [128, 512], F32))
                            for i in range(2)]
                        p_tra = es.enter_context(
                            nc.psum_tensor(f"a_ptra{s}", [128, 8, 128], BF16))
                        p_z = [es.enter_context(
                            nc.psum_tensor(f"a_pz{s}_{i}", [128, 512], F32))
                            for i in range(2)]
                        p_ht = [es.enter_context(
                            nc.psum_tensor(f"a_pht{s}_{i}", [128, 512], F32))
                            for i in range(2)]
                        lva = [0, 0]
                        evA = 0
                        vbf = 0
                        for q in range(TPS):
                            g = s * TPS + q
                            p = q % 2
                            ph.on("sync", lambda e, g=g, p=p: e.dma_start(
                                out=rtile[p][:],
                                in_=resid1[g * 128:(g + 1) * 128, :]),
                                waits=[(sv, lva[p])],
                                inc=sldp[p], inc_by=16)
                            ph.on("vector", lambda e, g=g, p=p: e.tensor_scalar(
                                out=xl_bf[:], in0=rtile[p][:],
                                scalar1=m1_all[:, g:g + 1],
                                scalar2=rstd1_all[:, g:g + 1],
                                op0=ALU.subtract, op1=ALU.mult),
                                waits=[(sldp[p], sldp[p].v), (st_, vbf)], inc=sv)
                            lva[p] = sv.v
                            xl_ready = sv.v
                            for k in range(NDC):
                                ph.on("tensor", lambda e, k=k: e.transpose(
                                    out=p_tra[:, k, :],
                                    in_=xl_bf[:, k * 128:(k + 1) * 128],
                                    identity=identb[:]),
                                    waits=([(sv, xl_ready), (ss, evA)]
                                           if k == 0 else ()),
                                    inc=(st_ if k == NDC - 1 else None))
                            vbf = st_.v
                            ph.on("scalar", lambda e, q=q: e.copy(
                                out=xlnT[:, 0:NDC, q * 128:(q + 1) * 128],
                                in_=p_tra[:]), waits=[(st_, st_.v)], inc=ss)
                            evA = ss.v
                        xlnT_ready = ss.v
                        if dbg is not None and s == DBG_SEQ:
                            ph.on("gpsimd", lambda e: e.dma_start(
                                out=dbg["xlnT"][:], in_=xlnT[:]),
                                waits=[(ss, xlnT_ready)], inc=sstp[0], inc_by=16)
                        zfree = [0, 0]
                        zuse = [0, 0]
                        htfree = [0, 0]
                        last_scan = 0
                        for c in range(NDC):
                            sa_done = 0
                            for sl in range(4):
                                bp = sl % 2
                                for k in range(NDC):
                                    ph.on("tensor", lambda e, k=k, c=c, sl=sl, bp=bp: e.matmul(
                                        out=p_z[bp][:], lhsT=wzb[:, k, c * 128:(c + 1) * 128],
                                        rhs=xlnT[:, k, sl * 512:(sl + 1) * 512],
                                        start=(k == 0), stop=(k == NDC - 1)),
                                        waits=([(ss, xlnT_ready), (ss, zfree[bp])]
                                               if k == 0 else ()),
                                        inc=(st_ if k == NDC - 1 else None))
                                z_done = st_.v
                                for k in range(NDC):
                                    ph.on("tensor", lambda e, k=k, c=c, sl=sl, bp=bp: e.matmul(
                                        out=p_ht[bp][:], lhsT=whb[:, k, c * 128:(c + 1) * 128],
                                        rhs=xlnT[:, k, sl * 512:(sl + 1) * 512],
                                        start=(k == 0), stop=(k == NDC - 1)),
                                        waits=([(sv, htfree[bp])] if k == 0 else ()),
                                        inc=(st_ if k == NDC - 1 else None))
                                ht_done = st_.v
                                ph.on("scalar", lambda e, bp=bp, c=c: e.activation(
                                    out=z_sb[bp][:], in_=p_z[bp][:],
                                    func=ACT.Sigmoid, bias=bz_s[:, c:c + 1]),
                                    waits=[(st_, z_done), (sv, zuse[bp])], inc=ss)
                                zs_done = ss.v
                                ph.on("scalar", lambda e, bp=bp, c=c, sl=sl: e.activation(
                                    out=a_buf[:, sl * 512:(sl + 1) * 512],
                                    in_=p_z[bp][:], func=ACT.Sigmoid,
                                    bias=nbz_s[:, c:c + 1], scale=-1.0),
                                    waits=[(sv, last_scan)], inc=ss)
                                zfree[bp] = ss.v
                                sa_done = ss.v
                                ph.on("vector", lambda e, bp=bp, c=c, sl=sl: e.scalar_tensor_tensor(
                                    out=bb_buf[:, sl * 512:(sl + 1) * 512],
                                    in0=p_ht[bp][:], scalar=bh_s[:, c:c + 1],
                                    in1=z_sb[bp][:], op0=ALU.add, op1=ALU.mult),
                                    waits=[(ss, zs_done), (st_, ht_done)], inc=sv)
                                htfree[bp] = sv.v
                                zuse[bp] = sv.v
                            if dbg is not None and s == DBG_SEQ and c == 0:
                                av = sa_done
                                bv = sv.v
                                ph.on("gpsimd", lambda e: e.dma_start(
                                    out=dbg["abuf"][:], in_=a_buf[:]),
                                    waits=[(ss, av), (sv, bv)], inc=sstp[0], inc_by=16)
                                ph.on("gpsimd", lambda e: e.dma_start(
                                    out=dbg["bbuf"][:], in_=bb_buf[:]),
                                    inc=sstp[0], inc_by=16)
                                ph.wait("vector", [(sstp[0], sstp[0].v)])
                            ph.on("vector", lambda e, c=c: e.tensor_tensor_scan(
                                out=h_t[:, c, :], data0=a_buf[:], data1=bb_buf[:],
                                initial=0.0, op0=ALU.mult, op1=ALU.add),
                                waits=[(ss, sa_done)], inc=sv)
                            last_scan = sv.v
                            if dbg is not None and s == DBG_SEQ and c == NDC - 1:
                                ph.on("gpsimd", lambda e: e.dma_start(
                                    out=dbg["wgb15"][:], in_=wgb[:]),
                                    waits=[(sv, sv.v)], inc=sstp[0], inc_by=16)
                            if dbg is not None and s == DBG_SEQ and c == 0:
                                ph.on("gpsimd", lambda e: e.dma_start(
                                    out=dbg["h0"][:], in_=h_t[:, 0, :]),
                                    waits=[(sv, sv.v)], inc=sstp[0], inc_by=16)
                        if dbg is not None and s == DBG_SEQ:
                            ph.on("gpsimd", lambda e: e.dma_start(
                                out=dbg["hall"][:], in_=h_t[:]),
                                waits=[(sv, sv.v)], inc=sstp[0], inc_by=16)
                        ph.wait("gpsimd", [(sstp[0], sstp[0].v)])
                        ph.run(nc)

                    with ExitStack() as es:
                        ph = Ph()
                        hb_sl = es.enter_context(
                            nc.sbuf_tensor(f"c_hb{s}", [128, NDC, 512], BF16))
                        g_sb = [es.enter_context(
                            nc.sbuf_tensor(f"c_g{s}_{i}", [128, 512], F32))
                            for i in range(2)]
                        ao = es.enter_context(
                            nc.sbuf_tensor(f"c_ao{s}", [128, NDC, 512], F32))
                        rtile = [es.enter_context(
                            nc.sbuf_tensor(f"c_rt{s}_{i}", [128, D], F32))
                            for i in range(2)]
                        rout2 = [es.enter_context(
                            nc.sbuf_tensor(f"c_ro{s}_{i}", [128, D], F32))
                            for i in range(2)]
                        x2c = es.enter_context(
                            nc.sbuf_tensor(f"c_x2_{s}", [128, D], F32))
                        p_gt = [es.enter_context(
                            nc.psum_tensor(f"c_pgt{s}_{i}", [128, 512], F32))
                            for i in range(2)]
                        p_tao = [es.enter_context(
                            nc.psum_tensor(f"c_ptao{s}_{i}", [128, 4, 128], F32))
                            for i in range(2)]
                        scan_all = sv.v
                        gfree = [0, 0]
                        hb_free = 0
                        ao_free = 0
                        tao_free = [0, 0]
                        lvc = [0, 0]
                        stc = [0, 0]
                        for sl in range(4):
                            for k in range(NDC):
                                ph.on("scalar", lambda e, k=k, sl=sl: e.copy(
                                    out=hb_sl[:, k, :],
                                    in_=h_t[:, k, sl * 512:(sl + 1) * 512]),
                                    waits=([(sv, scan_all), (st_, hb_free)]
                                           if k == 0 else ()), inc=ss)
                            hb_ready = ss.v
                            if dbg is not None and s == DBG_SEQ and sl == 0:
                                ph.on("gpsimd", lambda e: e.dma_start(
                                    out=dbg["hb"][:], in_=hb_sl[:]),
                                    waits=[(ss, hb_ready)], inc=sstp[0], inc_by=16)
                            for m in range(NDC):
                                bp = m % 2
                                for k in range(NDC):
                                    ph.on("tensor", lambda e, k=k, m=m, bp=bp: e.matmul(
                                        out=p_gt[bp][:], lhsT=wgb[:, k, m * 128:(m + 1) * 128],
                                        rhs=hb_sl[:, k, :],
                                        start=(k == 0), stop=(k == NDC - 1)),
                                        waits=([(ss, hb_ready), (sv, gfree[bp])]
                                               if k == 0 else ()),
                                        inc=(st_ if k == NDC - 1 else None))
                                gt_done = st_.v
                                if m == NDC - 1:
                                    hb_free = st_.v
                                ph.on("scalar", lambda e, bp=bp, m=m: e.activation(
                                    out=g_sb[bp][:], in_=p_gt[bp][:],
                                    func=ACT.Sigmoid, bias=bg_s[:, m:m + 1]),
                                    waits=[(st_, gt_done)], inc=ss)
                                gs_done = ss.v
                                ph.on("vector", lambda e, bp=bp, m=m, sl=sl: e.tensor_tensor(
                                    out=ao[:, m, :], in0=g_sb[bp][:],
                                    in1=h_t[:, m, sl * 512:(sl + 1) * 512],
                                    op=ALU.mult),
                                    waits=[(ss, gs_done), (st_, ao_free)], inc=sv)
                                gfree[bp] = sv.v
                                if dbg is not None and s == DBG_SEQ and sl == 0:
                                    ph.on("gpsimd", lambda e, m=m: e.dma_start(
                                        out=dbg["g"][:, m, :], in_=g_sb[bp][:]),
                                        waits=[(ss, gs_done)], inc=sstp[0], inc_by=16)
                                    ph.wait("scalar", [(sstp[0], sstp[0].v)])
                            ao_ready = sv.v
                            if dbg is not None and s == DBG_SEQ and sl == 0:
                                ph.on("gpsimd", lambda e: e.dma_start(
                                    out=dbg["ao"][:], in_=ao[:]),
                                    waits=[(sv, ao_ready)], inc=sstp[0], inc_by=16)
                            for q in range(4):
                                g = s * TPS + sl * 4 + q
                                p = q % 2
                                ph.on("sync", lambda e, g=g, p=p: e.dma_start(
                                    out=rtile[p][:],
                                    in_=resid1[g * 128:(g + 1) * 128, :]),
                                    waits=[(sv, lvc[p])], inc=sldp[p], inc_by=16)
                                ld_here = sldp[p].v
                                for m in range(NDC):
                                    ph.on("tensor", lambda e, m=m, q=q: e.matmul(
                                        out=p_tao[m // 4][:, m % 4, :],
                                        lhsT=ao[:, m, q * 128:(q + 1) * 128],
                                        rhs=identf[:], start=True, stop=True),
                                        waits=([(sv, ao_ready),
                                                (sv, tao_free[0]),
                                                (sv, tao_free[1])]
                                               if m == 0 else ()),
                                        inc=(st_ if m == NDC - 1 else None))
                                if q == 3:
                                    ao_free = st_.v
                                tao_done = st_.v
                                if dbg is not None and s == DBG_SEQ and sl == 0 and q == 0:
                                    ph.on("vector", lambda e: e.tensor_copy(
                                        out=rout2[1][:, 0:512],
                                        in_=p_tao[0][:].rearrange("p a b -> p (a b)")),
                                        waits=[(st_, tao_done)], inc=sv)
                                    ph.on("vector", lambda e: e.tensor_copy(
                                        out=rout2[1][:, 512:1024],
                                        in_=p_tao[1][:].rearrange("p a b -> p (a b)")), inc=sv)
                                    ph.on("gpsimd", lambda e: e.dma_start(
                                        out=dbg["taoT"][:].rearrange("p a b c -> p (a b c)"),
                                        in_=rout2[1][:]),
                                        waits=[(sv, sv.v)], inc=sstp[0], inc_by=16)
                                    ph.wait("vector", [(sstp[0], sstp[0].v)])
                                ph.wait("vector", [(sstp[p], stc[p])])
                                for half2 in range(2):
                                    ph.on("vector", lambda e, half2=half2, p=p: e.tensor_tensor(
                                        out=rout2[p][:, half2 * 512:(half2 + 1) * 512],
                                        in0=p_tao[half2][:].rearrange("p a b -> p (a b)"),
                                        in1=rtile[p][:, half2 * 512:(half2 + 1) * 512],
                                        op=ALU.add),
                                        waits=([(st_, tao_done), (sldp[p], ld_here)]
                                               if half2 == 0 else ()), inc=sv)
                                lvc[p] = sv.v
                                tao_free[0] = sv.v
                                tao_free[1] = sv.v
                                rdone = emit_tile_stats(
                                    ph, sm, x2c, sums2, sqs2, g, rout2[p][:],
                                    wait_src=[(sv, sv.v)])
                                ph.on("gpsimd", lambda e, g=g, p=p: e.dma_start(
                                    out=resid2[g * 128:(g + 1) * 128, :],
                                    in_=rout2[p][:]),
                                    waits=[(sv, rdone)], inc=sstp[p], inc_by=16)
                                stc[p] = sstp[p].v
                        if dbg is not None and s == SEQ_PC - 1:
                            ph.on("gpsimd", lambda e: e.dma_start(
                                out=dbg["wzb2"][:], in_=wzb[:]), inc=sstp[0], inc_by=16)
                            ph.on("gpsimd", lambda e: e.dma_start(
                                out=dbg["wgb2"][:], in_=wgb[:]), inc=sstp[0], inc_by=16)
                        ph.wait("gpsimd", [(sstp[0], sstp[0].v), (sstp[1], sstp[1].v)])
                        ph.run(nc)

        with ExitStack() as esw:
            wgT = esw.enter_context(nc.sbuf_tensor("wgT2", [128, NDC, II], BF16))
            wuT = esw.enter_context(nc.sbuf_tensor("wuT2", [128, NDC, II], BF16))
            wdT = esw.enter_context(nc.sbuf_tensor("wdT2", [128, NIC, D], BF16))
            with ExitStack() as es:
                ph = Ph()
                ph.wait("sync", [(smisc, sm["stash_done"])])
                ph.on("sync", lambda e: e.dma_start(
                    out=wgT[:], in_=wg_st.rearrange("(k p) i -> p k i", p=128)),
                    inc=smisc, inc_by=16)
                ph.on("sync", lambda e: e.dma_start(
                    out=wuT[:], in_=wu_st.rearrange("(k p) i -> p k i", p=128)),
                    inc=smisc, inc_by=16)
                ph.on("sync", lambda e: e.dma_start(
                    out=wdT[:], in_=wd_st.rearrange("(k p) i -> p k i", p=128)),
                    inc=smisc, inc_by=16)
                fb2 = mk_fin_bufs(nc, es, "g_", NT)
                fv2 = emit_stats_final(ph, sm, fb2, sums2, sqs2, m2_all,
                                       veps2_all, NT, sv.v)
                emit_rstd(ph, sm, veps2_all, rstd2_all, scr_all, 0, NT, fv2)
                ph.wait("tensor", [(smisc, smisc.v)])
                ph.run(nc)
            build_ffn_phase(nc, sm, resid2, out_d, wgT[:], wuT[:], wdT[:],
                            m2_all, rstd2_all, identb, "f3_", out_stats=None)

        with ExitStack() as es:
            ph = Ph()
            ph.wait("gpsimd", [(sstp[0], sstp[0].v), (sstp[1], sstp[1].v),
                               (smisc, smisc.v)])
            ph.run(nc)

    return nc


_NC_CACHE = {}


def _get_nc(debug=False):
    if debug not in _NC_CACHE:
        _NC_CACHE[debug] = build_program(debug)
    return _NC_CACHE[debug]


def make_in_maps(L0_attn_x, L0_attn_Wz, L0_attn_bz, L0_attn_Wh, L0_attn_bh,
                 L0_attn_Wg, L0_attn_bg, L0_ffn_w_gate, L0_ffn_w_up,
                 L0_ffn_w_down):
    x = np.ascontiguousarray(np.asarray(L0_attn_x, dtype=np.float32))
    wz2 = np.ascontiguousarray(
        np.transpose(np.asarray(L0_attn_Wz, np.float32), (1, 0, 2)).reshape(D, D))
    wh2 = np.ascontiguousarray(
        np.transpose(np.asarray(L0_attn_Wh, np.float32), (1, 0, 2)).reshape(D, D))
    wgd = np.ascontiguousarray(np.asarray(L0_attn_Wg, np.float32))
    bzf = np.ascontiguousarray(np.asarray(L0_attn_bz, np.float32).reshape(D))
    bhf = np.ascontiguousarray(np.asarray(L0_attn_bh, np.float32).reshape(D))
    bgf = np.ascontiguousarray(np.asarray(L0_attn_bg, np.float32).reshape(D))
    wgate = np.ascontiguousarray(np.asarray(L0_ffn_w_gate, np.float32))
    wup = np.ascontiguousarray(np.asarray(L0_ffn_w_up, np.float32))
    wdown = np.ascontiguousarray(np.asarray(L0_ffn_w_down, np.float32))
    identb = np.eye(128, dtype=ml_dtypes.bfloat16)
    identf = np.eye(128, dtype=np.float32)
    shards = x.reshape(NCORES, TOK, D)
    return [{"x_in": np.ascontiguousarray(shards[i]), "wz2": wz2, "wh2": wh2,
             "wgd": wgd, "bzf": bzf, "bhf": bhf, "bgf": bgf, "wgate": wgate,
             "wup": wup, "wdown": wdown, "identb_d": identb,
             "identf_d": identf} for i in range(NCORES)]


def kernel(**inputs):
    nc = _get_nc(False)
    in_maps = make_in_maps(**inputs)
    res = run_bass_kernel_spmd(nc, in_maps, core_ids=list(range(NCORES)))
    out = np.stack([np.asarray(res.results[i]["out"]) for i in range(NCORES)], 0)
    return out.reshape(B, S, D).astype(np.float32)
